# version 18
# baseline (speedup 1.0000x reference)
import contextlib
import numpy as np
import concourse.bass as bass
import concourse.mybir as mybir
from concourse.bass_core_rust import IndirectOffsetOnAxis
from concourse.bass_utils import run_bass_kernel_spmd

F32 = mybir.dt.float32
BF16 = mybir.dt.bfloat16
AF = mybir.ActivationFunctionType
ALU = mybir.AluOpType
AX = mybir.AxisListType

S, D, H, DH, E, FF = 2048, 1024, 16, 64, 8, 4096
NT = 16
KC = 8
LAM = 0.99
NCORE = 8
NG = 2
FCG = 16
CAP = 1024
GT = CAP // 128
BIGF = 1.0e9

ESEM = {"pe": "pe", "dv": "dv", "ac": "ac", "sy": "qX", "gp": "qW"}


class Buf:
    __slots__ = ("w", "r", "war")

    def __init__(self):
        self.w, self.r, self.war = [], [], []

    def rotate(self):
        self.war = list({t for t in self.w + self.r + self.war})
        self.w, self.r = [], []

    def inherit(self, others):
        for o in others:
            self.war += o.w + o.r + o.war


class Sched:
    def __init__(self):
        self.ops = []
        self.cnt = {}
        self.pending = {}

    def _tok(self, sem, n):
        c = self.cnt.get(sem, 0) + n
        self.cnt[sem] = c
        return (sem, c)

    def op(self, eng, fn, reads=(), writes=(), waits=(), n=None, chan=None):
        sem = chan or ESEM[eng]
        if n is None:
            n = 16 if eng in ("sy", "gp") else 1
        wl = list(waits)
        for b in reads:
            wl += b.w
        for b in writes:
            wl += b.war + b.r
        wl += self.pending.pop(eng, [])
        tok = self._tok(sem, n)
        self.ops.append((eng, fn, wl, (sem, n)))
        for b in reads:
            b.r.append(tok)
        for b in writes:
            b.w.append(tok)
        return tok

    def group(self, eng, fns, reads=(), writes=(), waits=()):
        sem = ESEM[eng]
        wl = list(waits)
        for b in reads:
            wl += b.w
        for b in writes:
            wl += b.war + b.r
        wl += self.pending.pop(eng, [])
        tok = self._tok(sem, 1)
        nf = len(fns)
        for i, fn in enumerate(fns):
            self.ops.append((eng, fn, wl if i == 0 else [],
                             (sem, 1) if i == nf - 1 else None))
        for b in reads:
            b.r.append(tok)
        for b in writes:
            b.w.append(tok)
        return tok

    def barrier(self, engines=("pe", "dv", "ac", "sy", "gp")):
        snap = list(self.cnt.items())
        for e in engines:
            self.pending.setdefault(e, []).extend(snap)


def emit(block, sched, sems):
    streams = {e: [] for e in ("pe", "dv", "ac", "sy", "gp")}
    for rec in sched.ops:
        streams[rec[0]].append(rec)
    for e, wl in list(sched.pending.items()):
        if wl:
            streams[e].append((e, lambda eng: eng.nop(), wl, None))
    sched.pending = {}

    def run(eng_obj, recs):
        maxw = {}
        for _, fn, waits, inc in recs:
            need = {}
            for sname, v in waits:
                if need.get(sname, 0) < v:
                    need[sname] = v
            for sname, v in need.items():
                if maxw.get(sname, 0) < v:
                    maxw[sname] = v
                    eng_obj.wait_ge(sems[sname], v)
            inst = fn(eng_obj)
            if inc is not None:
                inst.then_inc(sems[inc[0]], inc[1])

    block.tensor(lambda t: run(t, streams["pe"]))
    block.vector(lambda v: run(v, streams["dv"]))
    block.scalar(lambda a: run(a, streams["ac"]))
    block.sync(lambda sy: run(sy, streams["sy"]))
    block.gpsimd(lambda g: run(g, streams["gp"]))


def build_nc(dbg=()):
    nc = bass.Bass("TRN2", target_bir_lowering=False)

    def dt(name, shape, dtype=F32, kind="ExternalInput"):
        return nc.dram_tensor(name, shape, dtype, kind=kind).ap()

    x_d = dt("x", [S, D])
    Wq_d = dt("Wq", [D, D]); Wk_d = dt("Wk", [D, D]); Wv_d = dt("Wv", [D, D])
    Wg_d = dt("Wg", [D, D]); Wo_d = dt("Wo", [D, D])
    bq_d = dt("bq", [1, D]); bk_d = dt("bk", [1, D])
    bv_d = dt("bv", [1, D]); bo_d = dt("bo", [1, D])
    gam_d = dt("gam", [128, 8]); bet_d = dt("bet", [128, 8]); bgc_d = dt("bgc", [128, 8])
    rw_d = dt("rw", [128, 8, 8])
    b1c_d = dt("b1c", [128, 8, 32])
    b2_d = dt("b2", [8, D])
    W1r_d = dt("W1r", [E, 32, 128, 8, 128])
    W2_d = dt("W2", [E, FF, D])
    id_d = dt("ident", [128, 128])
    tri_d = dt("trils", [128, 128])
    on_d = dt("onesf", [128, 128])
    ce_d = dt("ce", [128, 16, 8])
    tok_d = dt("tokid", [128, 16])
    xg_d = dt("xg", [E * CAP, 1028], F32, "Internal")
    acc_d = dt("accd", [S, D], F32, "Internal")
    ca_d = dt("causal", [128, 128])
    lamq_d = dt("lamq", [128, 16])
    lamk_d = dt("lamk", [128, 16])
    out_d = dt("out", [S, D], F32, "ExternalOutput")
    x1sp_d = dt("x1sp", [S, D], F32, "Internal")
    dbg_d = {name: dt("dbg_" + name, [S, D], F32, "ExternalOutput") for name in dbg}

    s = Sched()
    B = Buf

    with contextlib.ExitStack() as es:
        def sb(name, shape, dtype, side="left"):
            return es.enter_context(nc.sbuf_tensor(name, shape, dtype, side=side))

        id_sb = sb("id_sb", [128, 128], BF16)
        ca_sb = sb("ca_sb", [128, 128], BF16)
        ones = sb("ones", [1, 128], BF16)
        lamq = sb("lamq_sb", [128, 16], F32)
        lamk = sb("lamk_sb", [128, 16], F32)
        eps = sb("eps_sb", [128, 1], F32)
        gam = sb("gam_sb", [128, 8], F32)
        bet = sb("bet_sb", [128, 8], F32)
        bgc = sb("bgc_sb", [128, 8], F32)
        bq_r = sb("bq_r", [1, D], BF16)
        bk_r = sb("bk_r", [1, D], BF16)
        bv_r = sb("bv_r", [1, D], BF16)
        bo_r = sb("bo_r", [1, D], BF16)
        b1c = sb("b1c_sb", [128, 8, 32], F32)
        b2sb = sb("b2_sb", [8, D], BF16)
        rw = sb("rw_sb", [128, 8, 8], BF16)
        Ssb = sb("Ssb", [64, 1024], BF16)
        x1T = sb("x1T", [128, 8, S], BF16)
        w_sb = sb("w_sb", [128, 16, 8], F32)
        stx = sb("stx", [128, 6, 4], F32)
        bnx = sb("bnx", [128, 4, 12], F32)
        tri_sb = sb("tri_sb", [128, 128], BF16)
        id32 = sb("id32", [128, 128], F32)
        on_sb = sb("on_sb", [128, 128], BF16)
        ce16 = sb("ce16", [128, 16, 8], F32)
        tok_sb = sb("tok_sb", [128, 16], F32)
        zrow = sb("zrow", [128, 1028], F32)
        flg = sb("flg", [128, 16, 8], BF16)

        bc_xg = es.enter_context(nc.gpsimd.register("bc_xg"))
        bc_ac = es.enter_context(nc.gpsimd.register("bc_ac"))

        consts_b = B()
        ssb_b = B()
        x1T_b = B()
        w_b = [B() for _ in range(NT)]
        stx_b = [B() for _ in range(6)]
        bnx_b = [B() for _ in range(4)]
        _stc = [0]

        def stat_slot():
            sl, bl = _stc[0] % 6, _stc[0] % 4
            _stc[0] += 1
            return sl, bl

        for dst_t, src in ((id_sb, id_d), (ca_sb, ca_d), (b2sb, b2_d),
                           (tri_sb, tri_d), (on_sb, on_d),
                           (bq_r, bq_d), (bk_r, bk_d), (bv_r, bv_d), (bo_r, bo_d),
                           (rw, rw_d)):
            s.op("gp", lambda g, a=dst_t, bb=src: g.dma_start(a[:], bb),
                 writes=[consts_b], chan="cG")
        for dst_t, src in ((lamq, lamq_d), (lamk, lamk_d), (gam, gam_d),
                           (bet, bet_d), (bgc, bgc_d), (b1c, b1c_d),
                           (ce16, ce_d), (tok_sb, tok_d), (id32, id_d)):
            s.op("sy", lambda g, a=dst_t, bb=src: g.dma_start(a[:], bb),
                 writes=[consts_b], chan="cS")
        s.op("gp", lambda g: g.reg_mov(bc_xg, E * CAP - 1), chan="gpr", n=1)
        s.op("gp", lambda g: g.reg_mov(bc_ac, S - 1), chan="gpr", n=1)
        s.op("dv", lambda v: v.memset(ones[:], 1.0), writes=[consts_b])
        s.op("dv", lambda v: v.memset(eps[:], 1e-5), writes=[consts_b])
        z0_tok = s.op("dv", lambda v: v.memset(zrow[:], 0.0), writes=[consts_b])
        s.op("dv", lambda v: v.memset(
            zrow[:, 1027:1028], 2048.0), writes=[consts_b],
            waits=[z0_tok])
        zf_toks = []

        with contextlib.ExitStack() as esA:
            def sbA(name, shape, dtype):
                return esA.enter_context(nc.sbuf_tensor(name, shape, dtype, side="right"))

            xtile = sbA("xtile", [128, 2, 1024], F32)
            xbf = sbA("xbf", [128, 1024], BF16)
            xTh = sbA("xTh", [128, 8, 1024], BF16)
            Vtok = sbA("Vtok", [128, 8, 1024], BF16)
            Qtok = sbA("Qtok", [128, 8, 1024], BF16)
            Ktok = sbA("Ktok", [128, 8, 1024], BF16)
            Gt = sbA("Gt", [128, 8, 1024], BF16)
            Wa = sbA("Wa", [128, 8, 1024], BF16)
            Wb = sbA("Wb", [128, 8, 1024], BF16)
            qf = sbA("qf", [64, 16, 128], BF16)
            kf = sbA("kf", [64, 16, 128], BF16)
            at = sbA("at", [128, 16, 128], BF16)
            rn = sbA("rn", [128, 1024], BF16)
            rnT = sbA("rnT", [128, 8, 128], BF16)
            ngT = sbA("ngT", [128, 8, 128], BF16)
            sum_sb = sbA("sum_sb", [128, 1024], F32)
            x1f = sbA("x1f", [128, 1024], F32)
            x1b = sbA("x1b", [128, 1024], BF16)

            STp = esA.enter_context(nc.psum_tensor("STp", [64, 1024], F32))
            Tp = esA.enter_context(nc.psum_tensor("Tp", [128, 2, 8, 128], BF16))
            PP = esA.enter_context(nc.psum_tensor("PP", [128, 2, 1024], F32))

            def SCap(sl):
                return PP[:, 0, 128 * sl:128 * (sl + 1)]

            RWap = PP[:, 1, :]

            xt_b = [B(), B()]
            xbf_b = B()
            xTh_b = [B() for _ in range(8)]
            vt_b = [B() for _ in range(8)]
            qt_b = [B() for _ in range(8)]
            kt_b = [B() for _ in range(8)]
            gt_b = [B() for _ in range(8)]
            wa_b, wb_b = B(), B()
            qf_b, kf_b, at_b = B(), B(), B()
            rn_b, rnT_b, ngT_b = B(), B(), B()
            sum_b, x1f_b, x1b_b = B(), B(), B()
            st_b = B()
            tp_b = [B(), B()]
            pp_b = [B(), B()]
            sc_b = [B(), B()]
            rw_bk = [B(), B()]
            x1sp_tok = [None] * NT

            for hf in range(2):
                wa_b.rotate(); wb_b.rotate()
                for k in range(KC):
                    s.op("gp", lambda g, k=k: g.dma_start(
                        Wa[:, k, :], Wv_d[128 * k:128 * (k + 1), :]),
                        writes=[wa_b], chan="wV")
                for k in range(KC):
                    s.op("gp", lambda g, k=k: g.dma_start(
                        Wb[:, k, :], Wg_d[128 * k:128 * (k + 1), :]),
                        writes=[wb_b], chan="wG")

                for tt in range(8):
                    t = 8 * hf + tt
                    sl = tt % 2
                    xt_b[sl].rotate()
                    s.op("sy", lambda g, t=t, sl=sl: g.dma_start(
                        xtile[:, sl, :], x_d[128 * t:128 * (t + 1), :]),
                        writes=[xt_b[sl]], chan="xt%d" % sl)
                    xbf_b.rotate()
                    s.op("dv", lambda v, sl=sl: v.tensor_copy(xbf[:], xtile[:, sl, :]),
                         reads=[xt_b[sl]], writes=[xbf_b])
                    pb = tp_b[tt % 2]
                    pb.rotate()
                    s.group("pe", [
                        (lambda t_, j=j, pp=tt % 2: t_.transpose(
                            Tp[:, pp, j, :], xbf[:, 128 * j:128 * (j + 1)], id_sb[:]))
                        for j in range(8)],
                        reads=[xbf_b, consts_b], writes=[pb])
                    xTh_b[tt].rotate()
                    for j in range(8):
                        s.op("dv", lambda v, j=j, tt=tt, pp=tt % 2: v.tensor_copy(
                            xTh[:, j, 128 * tt:128 * (tt + 1)], Tp[:, pp, j, :]),
                            reads=[pb], writes=[xTh_b[tt]])
                    vsl = tt % 2
                    pp_b[vsl].rotate()
                    fns = [
                        (lambda t_, sl=vsl: t_.matmul(
                            PP[:, sl, 0:512], ones[:], bv_r[0:1, 0:512],
                            start=True, stop=False)),
                        (lambda t_, sl=vsl: t_.matmul(
                            PP[:, sl, 512:1024], ones[:], bv_r[0:1, 512:1024],
                            start=True, stop=False)),
                    ]
                    for k in range(KC):
                        for c in range(2):
                            fns.append(lambda t_, k=k, c=c, sl=vsl, tt=tt: t_.matmul(
                                PP[:, sl, 512 * c:512 * (c + 1)],
                                xTh[:, k, 128 * tt:128 * (tt + 1)],
                                Wa[:, k, 512 * c:512 * (c + 1)],
                                start=False, stop=(k == KC - 1)))
                    s.group("pe", fns, reads=[xTh_b[tt], wa_b, consts_b],
                            writes=[pp_b[vsl]])
                    vt_b[tt].rotate()
                    s.op("dv", lambda v, tt=tt, sl=vsl: v.tensor_copy(
                        Vtok[:, tt, :], PP[:, sl, :]),
                        reads=[pp_b[vsl]], writes=[vt_b[tt]])

                for j in range(8):
                    sl = j % 2
                    pp_b[sl].rotate()
                    fns = []
                    for k in range(KC):
                        for c in range(2):
                            fns.append(lambda t_, k=k, c=c, sl=sl, j=j: t_.matmul(
                                PP[:, sl, 512 * c:512 * (c + 1)],
                                Wb[:, k, 128 * j:128 * (j + 1)],
                                xTh[:, k, 512 * c:512 * (c + 1)],
                                start=(k == 0), stop=(k == KC - 1)))
                    s.group("pe", fns,
                            reads=[wb_b, consts_b] + [xTh_b[q] for q in range(8)],
                            writes=[pp_b[sl]])
                    gt_b[j].rotate()
                    for c in range(2):
                        s.op("ac", lambda a, j=j, c=c, sl=sl: a.activation(
                            Gt[:, j, 512 * c:512 * (c + 1)],
                            PP[:, sl, 512 * c:512 * (c + 1)],
                            AF.Sigmoid, bias=bgc[:, j:j + 1]),
                            reads=[pp_b[sl], consts_b], writes=[gt_b[j]])

                wa_b.rotate()
                for k in range(KC):
                    s.op("gp", lambda g, k=k: g.dma_start(
                        Wa[:, k, :], Wq_d[128 * k:128 * (k + 1), :]),
                        writes=[wa_b], chan="wQ")
                wb_b.rotate()
                for k in range(KC):
                    s.op("gp", lambda g, k=k: g.dma_start(
                        Wb[:, k, :], Wk_d[128 * k:128 * (k + 1), :]),
                        writes=[wb_b], chan="wK")

                for which in ("q", "k"):
                    if which == "q":
                        warena, wt, brow, lamv, dst, dst_b = wa_b, Wa, bq_r, lamq, Qtok, qt_b
                    else:
                        warena, wt, brow, lamv, dst, dst_b = wb_b, Wb, bk_r, lamk, Ktok, kt_b
                    for tt in range(8):
                        t = 8 * hf + tt
                        sl = tt % 2
                        pp_b[sl].rotate()
                        fns = [
                            (lambda t_, sl=sl, brow=brow: t_.matmul(
                                PP[:, sl, 0:512], ones[:], brow[0:1, 0:512],
                                start=True, stop=False)),
                            (lambda t_, sl=sl, brow=brow: t_.matmul(
                                PP[:, sl, 512:1024], ones[:], brow[0:1, 512:1024],
                                start=True, stop=False)),
                        ]
                        for k in range(KC):
                            for c in range(2):
                                fns.append(lambda t_, k=k, c=c, sl=sl, tt=tt, wt=wt: t_.matmul(
                                    PP[:, sl, 512 * c:512 * (c + 1)],
                                    xTh[:, k, 128 * tt:128 * (tt + 1)],
                                    wt[:, k, 512 * c:512 * (c + 1)],
                                    start=False, stop=(k == KC - 1)))
                        s.group("pe", fns, reads=[xTh_b[tt], warena, consts_b],
                                writes=[pp_b[sl]])
                        dst_b[tt].rotate()
                        for c in range(2):
                            s.op("ac", lambda a, tt=tt, c=c, sl=sl, t=t, dst=dst,
                                 lamv=lamv: a.activation(
                                     dst[:, tt, 512 * c:512 * (c + 1)],
                                     PP[:, sl, 512 * c:512 * (c + 1)],
                                     AF.Identity, scale=lamv[:, t:t + 1]),
                                 reads=[pp_b[sl], consts_b], writes=[dst_b[tt]])

                wa_b.rotate()
                for k in range(KC):
                    s.op("gp", lambda g, k=k: g.dma_start(
                        Wa[:, k, :], Wo_d[128 * k:128 * (k + 1), :]),
                        writes=[wa_b], chan="wO")

                for b_ in sc_b:
                    b_.inherit([pp_b[0]])
                for b_ in rw_bk:
                    b_.inherit([pp_b[1]])

                pend = [None]
                for tt in range(8):
                    i = 8 * hf + tt
                    t = i
                    sl = tt % 2
                    xt_b[sl].rotate()
                    xres_tok = s.op("sy", lambda g, t=t, sl=sl: g.dma_start(
                        xtile[:, sl, :], x_d[128 * t:128 * (t + 1), :]),
                        writes=[xt_b[sl]], chan="xt%d" % sl)
                    for zz in range(4):
                        zi = 4 * i + zz
                        zf_toks.append(s.op("sy", lambda g, zi=zi: g.dma_start(
                            xg_d[128 * zi:128 * (zi + 1), :], zrow[:]),
                            reads=[consts_b], chan="zf"))

                    qf_b.rotate()
                    for grp in range(2):
                        pb = tp_b[grp]
                        pb.rotate()
                        s.group("pe", [
                            (lambda t_, hh=hh, grp=grp, tt=tt: t_.transpose(
                                Tp[0:64, grp, hh, :],
                                Qtok[:, tt, 64 * (8 * grp + hh):64 * (8 * grp + hh) + 64],
                                id_sb[:]))
                            for hh in range(8)],
                            reads=[qt_b[tt], consts_b], writes=[pb])
                        for hh in range(8):
                            s.op("dv", lambda v, hh=hh, grp=grp: v.tensor_copy(
                                qf[:, 8 * grp + hh, :], Tp[0:64, grp, hh, :]),
                                reads=[pb], writes=[qf_b])
                    kf_b.rotate()
                    for grp in range(2):
                        pb = tp_b[grp]
                        pb.rotate()
                        s.group("pe", [
                            (lambda t_, hh=hh, grp=grp, tt=tt: t_.transpose(
                                Tp[0:64, grp, hh, :],
                                Ktok[:, tt, 64 * (8 * grp + hh):64 * (8 * grp + hh) + 64],
                                id_sb[:]))
                            for hh in range(8)],
                            reads=[kt_b[tt], consts_b], writes=[pb])
                        for hh in range(8):
                            s.op("ac", lambda a, hh=hh, grp=grp: a.copy(
                                kf[:, 8 * grp + hh, :], Tp[0:64, grp, hh, :]),
                                reads=[pb], writes=[kf_b])

                    st_b.rotate()
                    s.group("pe", [
                        (lambda t_, h=h, tt=tt, i=i: t_.matmul(
                            STp[0:64, 64 * h:64 * h + 64],
                            Ktok[:, tt, 64 * h:64 * h + 64],
                            Vtok[:, tt, 64 * h:64 * h + 64],
                            start=(i == 0 and h % 8 == 0),
                            stop=(i == NT - 1 and h == H - 1),
                            skip_group_check=True))
                        for h in range(H)],
                        reads=[kt_b[tt], vt_b[tt]], writes=[st_b])

                    at_b.rotate()
                    for hq in range(4):
                        bk = hq % 2
                        sc_b[bk].rotate()
                        s.group("pe", [
                            (lambda t_, h=h: t_.matmul(
                                SCap(h % 8), kf[:, h, :], qf[:, h, :],
                                start=True, stop=True))
                            for h in range(4 * hq, 4 * hq + 4)],
                            reads=[kf_b, qf_b], writes=[sc_b[bk]])
                        for h in range(4 * hq, 4 * hq + 4):
                            s.op("dv", lambda v, h=h: v.tensor_tensor(
                                at[:, h, :], SCap(h % 8), ca_sb[:], ALU.mult),
                                reads=[sc_b[bk], consts_b], writes=[at_b])

                    if pend[0] is not None:
                        pend[0]()
                        pend[0] = None
                    for b_ in rw_bk:
                        b_.rotate()
                    fns = [
                        (lambda t_, h=h, tt=tt: t_.matmul(
                            RWap[:, 64 * h:64 * h + 64], at[:, h, :],
                            Vtok[:, tt, 64 * h:64 * h + 64],
                            start=(h % 8 == 0), stop=False, skip_group_check=True))
                        for h in range(H)]
                    if i > 0:
                        fns += [
                            (lambda t_, h=h: t_.matmul(
                                RWap[:, 64 * h:64 * h + 64], qf[:, h, :],
                                Ssb[:, 64 * h:64 * h + 64],
                                start=False, stop=(h == H - 1), skip_group_check=True))
                            for h in range(H)]
                    s.group("pe", fns, reads=[at_b, vt_b[tt], qf_b, ssb_b],
                            writes=[rw_bk[0], rw_bk[1]])

                    ssb_b.rotate()
                    s.op("dv", lambda v: v.tensor_copy(Ssb[:], STp[0:64, :]),
                         reads=[st_b], writes=[ssb_b])

                    ssl, bsl = stat_slot()
                    stx_b[ssl].rotate(); bnx_b[bsl].rotate()
                    s.op("dv", lambda v, bsl=bsl: v.bn_stats(
                        bnx[:, bsl, 0:6], RWap[:, 0:512]),
                        reads=[rw_bk[0]], writes=[bnx_b[bsl]])
                    s.op("dv", lambda v, bsl=bsl: v.bn_stats(
                        bnx[:, bsl, 6:12], RWap[:, 512:1024]),
                        reads=[rw_bk[1]], writes=[bnx_b[bsl]])
                    s.op("dv", lambda v, ssl=ssl, bsl=bsl: v.bn_aggr(
                        stx[:, ssl, 0:2], bnx[:, bsl, 0:12]),
                        reads=[bnx_b[bsl]], writes=[stx_b[ssl]])
                    s.op("ac", lambda a, ssl=ssl: a.activation(
                        stx[:, ssl, 2:3], stx[:, ssl, 1:2], AF.Sqrt, bias=eps[:, 0:1]),
                        reads=[stx_b[ssl], consts_b], writes=[stx_b[ssl]])
                    s.op("dv", lambda v, ssl=ssl: v.reciprocal(
                        stx[:, ssl, 3:4], stx[:, ssl, 2:3]),
                        reads=[stx_b[ssl]], writes=[stx_b[ssl]])
                    rn_b.rotate()
                    s.op("dv", lambda v, ssl=ssl: v.tensor_scalar(
                        rn[:], RWap[:, :], stx[:, ssl, 0:1], stx[:, ssl, 3:4],
                        ALU.subtract, ALU.mult),
                        reads=[rw_bk[0], rw_bk[1], stx_b[ssl]], writes=[rn_b])

                    def mk_tail(tt=tt, i=i, t=t, sl=sl, xres_tok=xres_tok):
                        def _tail():
                            rnT_b.rotate()
                            for grp in range(2):
                                pb = tp_b[grp]
                                pb.rotate()
                                s.group("pe", [
                                    (lambda t_, jj=jj, grp=grp: t_.transpose(
                                        Tp[:, grp, jj, :],
                                        rn[:, 128 * (4 * grp + jj):128 * (4 * grp + jj) + 128],
                                        id_sb[:]))
                                    for jj in range(4)],
                                    reads=[rn_b, consts_b], writes=[pb])
                                for jj in range(4):
                                    j = 4 * grp + jj
                                    s.op("ac", lambda a, j=j, jj=jj, grp=grp: a.activation(
                                        rnT[:, j, :], Tp[:, grp, jj, :], AF.Identity,
                                        scale=gam[:, j:j + 1], bias=bet[:, j:j + 1]),
                                        reads=[pb, consts_b], writes=[rnT_b])
                            ngT_b.rotate()
                            for j in range(8):
                                s.op("dv", lambda v, j=j, tt=tt: v.tensor_tensor(
                                    ngT[:, j, :], rnT[:, j, :],
                                    Gt[:, j, 128 * tt:128 * (tt + 1)], ALU.mult),
                                    reads=[rnT_b, gt_b[j]], writes=[ngT_b])

                            for b_ in rw_bk:
                                b_.rotate()
                            fns = [
                                (lambda t_: t_.matmul(RWap[:, 0:512], ones[:], bo_r[0:1, 0:512],
                                                      start=True, stop=False)),
                                (lambda t_: t_.matmul(RWap[:, 512:1024], ones[:],
                                                      bo_r[0:1, 512:1024], start=True, stop=False)),
                            ]
                            for k in range(KC):
                                for c in range(2):
                                    fns.append(lambda t_, k=k, c=c: t_.matmul(
                                        RWap[:, 512 * c:512 * (c + 1)], ngT[:, k, :],
                                        Wa[:, k, 512 * c:512 * (c + 1)],
                                        start=False, stop=(k == KC - 1)))
                            s.group("pe", fns, reads=[ngT_b, wa_b, consts_b],
                                    writes=[rw_bk[0], rw_bk[1]])

                            sum_b.rotate()
                            for c in range(2):
                                s.op("dv", lambda v, c=c, sl=sl: v.tensor_tensor(
                                    sum_sb[:, 512 * c:512 * (c + 1)],
                                    RWap[:, 512 * c:512 * (c + 1)],
                                    xtile[:, sl, 512 * c:512 * (c + 1)], ALU.add),
                                    reads=[rw_bk[c], xt_b[sl]], writes=[sum_b],
                                    waits=[xres_tok])
                            ssl, bsl = stat_slot()
                            stx_b[ssl].rotate(); bnx_b[bsl].rotate()
                            s.op("dv", lambda v, bsl=bsl: v.bn_stats(
                                bnx[:, bsl, 0:6], sum_sb[:, 0:512]),
                                reads=[sum_b], writes=[bnx_b[bsl]])
                            s.op("dv", lambda v, bsl=bsl: v.bn_stats(
                                bnx[:, bsl, 6:12], sum_sb[:, 512:1024]),
                                reads=[sum_b], writes=[bnx_b[bsl]])
                            s.op("dv", lambda v, ssl=ssl, bsl=bsl: v.bn_aggr(
                                stx[:, ssl, 0:2], bnx[:, bsl, 0:12]),
                                reads=[bnx_b[bsl]], writes=[stx_b[ssl]])
                            s.op("ac", lambda a, ssl=ssl: a.activation(
                                stx[:, ssl, 2:3], stx[:, ssl, 1:2], AF.Sqrt, bias=eps[:, 0:1]),
                                reads=[stx_b[ssl], consts_b], writes=[stx_b[ssl]])
                            s.op("dv", lambda v, ssl=ssl: v.reciprocal(
                                stx[:, ssl, 3:4], stx[:, ssl, 2:3]),
                                reads=[stx_b[ssl]], writes=[stx_b[ssl]])
                            x1f_b.rotate()
                            s.op("dv", lambda v, ssl=ssl: v.tensor_scalar(
                                x1f[:], sum_sb[:], stx[:, ssl, 0:1], stx[:, ssl, 3:4],
                                ALU.subtract, ALU.mult),
                                reads=[sum_b, stx_b[ssl]], writes=[x1f_b])
                            x1sp_tok[t] = s.op("sy", lambda g, t=t: g.dma_start(
                                x1sp_d[128 * t:128 * (t + 1), :], x1f[:]),
                                reads=[x1f_b], chan="sp")
                            if "x1" in dbg_d:
                                s.op("sy", lambda g, t=t: g.dma_start(
                                    dbg_d["x1"][128 * t:128 * (t + 1), :], x1f[:]),
                                    reads=[x1f_b], chan="dbg")
                            x1b_b.rotate()
                            s.op("dv", lambda v: v.tensor_copy(x1b[:], x1f[:]),
                                 reads=[x1f_b], writes=[x1b_b])
                            for grp in range(2):
                                pb = tp_b[grp]
                                pb.rotate()
                                s.group("pe", [
                                    (lambda t_, jj=jj, grp=grp: t_.transpose(
                                        Tp[:, grp, jj, :],
                                        x1b[:, 128 * (4 * grp + jj):128 * (4 * grp + jj) + 128],
                                        id_sb[:]))
                                    for jj in range(4)],
                                    reads=[x1b_b, consts_b], writes=[pb])
                                for jj in range(4):
                                    k = 4 * grp + jj
                                    s.op("dv", lambda v, k=k, jj=jj, grp=grp, t=t: v.tensor_copy(
                                        x1T[:, k, 128 * t:128 * (t + 1)], Tp[:, grp, jj, :]),
                                        reads=[pb], writes=[x1T_b])
                        return _tail
                    pend[0] = mk_tail()
                if pend[0] is not None:
                    pend[0]()
                    pend[0] = None

        s.barrier()

        def sbF(name, shape, dtype):
            return es.enter_context(nc.sbuf_tensor(name, shape, dtype, side="right"))

        hT = sbF("hT", [128, FCG, CAP], BF16)
        W2g = sbF("W2g", [128, FCG, 1024], BF16)
        W1t = sbF("W1t", [128, 3, 8, 128], BF16)
        xga = sbF("xga", [128, 2, 1028], F32)
        xgT = sbF("xgT", [128, 8, CAP], BF16)
        accg = sbF("accg", [128, GT, 1024], F32)
        wgc = sbF("wgc", [128, GT], F32)
        idxg = sbF("idxg", [128, GT], mybir.dt.int32)
        xrel = sbF("xrel", [128, 2, 1024], F32)
        o_sb = sbF("o_sb", [128, 2, 1024], F32)
        wT = sbF("wT", [8, 2, 128], BF16)
        wbf = sbF("wbf", [128, 8], BF16)
        P8 = sbF("P8", [128, NT, 8], F32)
        T8 = sbF("T8", [128, NT, 8], F32)
        FF8 = sbF("FF8", [128, NT, 8], F32)
        D8 = sbF("D8", [128, NT, 8], F32)
        PF = sbF("PF", [128, NT, 8], F32)
        RT = sbF("RT", [128, NT, 8], F32)
        MT = sbF("MT", [128, NT, 4], F32)
        IX = sbF("IX", [128, NT, 2], mybir.dt.int32)
        dscr = sbF("dscr", [128, 4], F32)
        pay = sbF("pay", [128, 2, 1028], F32)

        hT_b = [B() for _ in range(FCG)]
        w2_b = B()
        w1_b = [B(), B(), B()]
        xga_b = [B(), B()]
        xgT_b = B()
        accg_b = [B() for _ in range(GT)]
        wgc_b, idxg_b = B(), B()
        xrel_b = [B(), B()]
        o_b = [B(), B()]
        wT_b = [B(), B()]
        wbf_b = B()
        p8_b, t8_b, ff_b, d8_b, pf_b = B(), B(), B(), B(), B()
        rt_b, mt_b, ix_b, dsc_b = B(), B(), B(), B()
        pay_b = [B(), B()]
        flg_b = B()
        scX_toks = []
        aci_toks = []
        scA_by_e = []

        with contextlib.ExitStack() as esR:
            PL = esR.enter_context(nc.psum_tensor("PL", [128, 2, 512], F32))
            Ppos = esR.enter_context(nc.psum_tensor("Ppos", [128, 2, 512], F32))
            PwT = esR.enter_context(nc.psum_tensor("PwT", [8, 128], BF16))
            Pb2 = esR.enter_context(nc.psum_tensor("Pb2", [128, 1024], F32))
            pl_b = [B(), B()]
            pp2_b = [B(), B()]
            pwt_b, pb2_b = B(), B()

            def bc8(ap):
                return ap.broadcast_to([128, NT, 8])

            for t in range(NT):
                r = t % 2
                pl_b[r].rotate()
                s.group("pe", [
                    (lambda t_, k=k, t=t, r=r: t_.matmul(
                        PL[:, r, 0:8], x1T[:, k, 128 * t:128 * (t + 1)], rw[:, k, :],
                        start=(k == 0), stop=(k == KC - 1)))
                    for k in range(KC)],
                    reads=[x1T_b, consts_b], writes=[pl_b[r]])
                s.op("dv", lambda v, t=t, r=r: v.tensor_copy(
                    P8[:, t, :], PL[:, r, 0:8]),
                    reads=[pl_b[r]], writes=[p8_b])
                if t < 2:
                    xrel_b[t].rotate()
                    s.op("sy", lambda g, t=t: g.dma_start(
                        xrel[:, t, :], x1sp_d[128 * t:128 * (t + 1), :]),
                        writes=[xrel_b[t]], waits=[x1sp_tok[t]],
                        chan="xr%d" % t)

            def dvb(fn, reads, writes):
                return s.op("dv", fn, reads=reads, writes=writes)

            RTs = lambda c: RT[:, :, c:c + 1]
            dvb(lambda v: v.reduce_max(RTs(0), P8[:], axis=AX.X), [p8_b], [rt_b])
            dvb(lambda v: v.tensor_tensor(T8[:], P8[:], bc8(RTs(0)), ALU.subtract),
                [p8_b, rt_b], [t8_b])
            s.op("ac", lambda a: a.activation(P8[:], T8[:], AF.Exp),
                 reads=[t8_b], writes=[p8_b])
            dvb(lambda v: v.reduce_sum(RTs(1), P8[:], axis=AX.X), [p8_b], [rt_b])
            dvb(lambda v: v.tensor_scalar(RTs(2), RTs(1), 1e-8, None, ALU.mult),
                [rt_b], [rt_b])
            dvb(lambda v: v.reduce_max(RTs(3), P8[:], axis=AX.X), [p8_b], [rt_b])
            dvb(lambda v: v.tensor_tensor(T8[:], P8[:], bc8(RTs(3)), ALU.is_lt),
                [p8_b, rt_b], [t8_b])
            dvb(lambda v: v.tensor_tensor(T8[:], T8[:], P8[:], ALU.mult),
                [t8_b, p8_b], [t8_b])
            dvb(lambda v: v.reduce_max(RTs(4), T8[:], axis=AX.X), [t8_b], [rt_b])
            dvb(lambda v: v.tensor_tensor(RTs(5), RTs(3), RTs(4), ALU.add),
                [rt_b], [rt_b])
            dvb(lambda v: v.tensor_tensor(RTs(6), RTs(5), RTs(2), ALU.add),
                [rt_b], [rt_b])
            dvb(lambda v: v.reciprocal(RTs(7), RTs(6)), [rt_b], [rt_b])
            dvb(lambda v: v.tensor_tensor(T8[:], P8[:], bc8(RTs(4)), ALU.is_ge),
                [p8_b, rt_b], [t8_b])
            dvb(lambda v: v.tensor_tensor(T8[:], T8[:], P8[:], ALU.mult),
                [t8_b, p8_b], [t8_b])
            for t in range(NT):
                w_b[t].rotate()
            dvb(lambda v: v.tensor_tensor(w_sb[:], T8[:], bc8(RTs(7)), ALU.mult),
                [t8_b, rt_b], w_b)
            dvb(lambda v: v.tensor_scalar(FF8[:], w_sb[:], 0.0, None, ALU.is_gt),
                w_b, [ff_b])
            flg_b.rotate()
            dvb(lambda v: v.tensor_copy(flg[:], FF8[:]), [ff_b], [flg_b])

            for t in range(NT):
                r = t % 2
                pp2_b[r].rotate()
                s.group("pe",
                        [(lambda t_, tp=tp, r=r: t_.matmul(
                            Ppos[:, r, 0:8], on_sb[:], flg[:, tp, :],
                            start=(tp == 0), stop=False, skip_group_check=True))
                         for tp in range(t)] +
                        [(lambda t_, t=t, r=r: t_.matmul(
                            Ppos[:, r, 0:8], tri_sb[:], flg[:, t, :],
                            start=(t == 0), stop=True, skip_group_check=True))],
                        reads=[flg_b, consts_b], writes=[pp2_b[r]])
                s.op("dv", lambda v, t=t, r=r: v.tensor_copy(
                    PF[:, t, :], Ppos[:, r, 0:8]),
                    reads=[pp2_b[r]], writes=[pf_b])

            dvb(lambda v: v.tensor_scalar(T8[:], PF[:], float(CAP), None, ALU.is_lt),
                [pf_b], [t8_b])
            dvb(lambda v: v.tensor_tensor(FF8[:], FF8[:], T8[:], ALU.mult),
                [ff_b, t8_b], [ff_b])
            dvb(lambda v: v.tensor_tensor(D8[:], PF[:], ce16[:], ALU.add),
                [pf_b, consts_b], [d8_b])
            dvb(lambda v: v.tensor_tensor(D8[:], D8[:], FF8[:], ALU.mult),
                [d8_b, ff_b], [d8_b])
            dvb(lambda v: v.tensor_scalar(T8[:], FF8[:], -BIGF, BIGF,
                                          ALU.mult, ALU.add),
                [ff_b], [t8_b])
            dvb(lambda v: v.tensor_tensor(D8[:], D8[:], T8[:], ALU.add),
                [d8_b, t8_b], [d8_b])
            dvb(lambda v: v.tensor_reduce(RTs(0), D8[:], axis=AX.X, op=ALU.min),
                [d8_b], [rt_b])
            dvb(lambda v: v.tensor_tensor(T8[:], D8[:], bc8(RTs(0)), ALU.is_equal),
                [d8_b, rt_b], [t8_b])
            dvb(lambda v: v.tensor_tensor(FF8[:], T8[:], ce16[:], ALU.mult),
                [t8_b, consts_b], [ff_b])
            dvb(lambda v: v.reduce_sum(RTs(4), FF8[:], axis=AX.X), [ff_b], [rt_b])
            dvb(lambda v: v.tensor_scalar(RTs(4), RTs(4), 1.0 / CAP, None, ALU.mult),
                [rt_b], [rt_b])
            dvb(lambda v: v.tensor_tensor(FF8[:], T8[:], w_sb[:], ALU.mult),
                [t8_b] + w_b, [ff_b])
            dvb(lambda v: v.reduce_sum(RTs(1), FF8[:], axis=AX.X), [ff_b], [rt_b])
            dvb(lambda v: v.tensor_tensor(T8[:], D8[:], bc8(RTs(0)), ALU.is_gt),
                [d8_b, rt_b], [t8_b])
            dvb(lambda v: v.tensor_tensor(FF8[:], D8[:], T8[:], ALU.mult),
                [d8_b, t8_b], [ff_b])
            dvb(lambda v: v.tensor_scalar(T8[:], T8[:], -BIGF, BIGF,
                                          ALU.mult, ALU.add),
                [t8_b], [t8_b])
            dvb(lambda v: v.tensor_tensor(FF8[:], FF8[:], T8[:], ALU.add),
                [ff_b, t8_b], [ff_b])
            dvb(lambda v: v.tensor_reduce(RTs(2), FF8[:], axis=AX.X, op=ALU.min),
                [ff_b], [rt_b])
            dvb(lambda v: v.tensor_tensor(T8[:], D8[:], bc8(RTs(2)), ALU.is_equal),
                [d8_b, rt_b], [t8_b])
            dvb(lambda v: v.tensor_tensor(FF8[:], T8[:], w_sb[:], ALU.mult),
                [t8_b] + w_b, [ff_b])
            dvb(lambda v: v.reduce_sum(RTs(3), FF8[:], axis=AX.X), [ff_b], [rt_b])
            s.op("dv", lambda v: v.tensor_copy(IX[:, :, 0:1], RTs(0)),
                 reads=[rt_b], writes=[ix_b])
            s.op("dv", lambda v: v.tensor_copy(IX[:, :, 1:2], RTs(2)),
                 reads=[rt_b], writes=[ix_b])
            s.op("dv", lambda v: v.tensor_copy(MT[:, :, 0:1], RTs(1)),
                 reads=[rt_b], writes=[mt_b])
            s.op("dv", lambda v: v.tensor_copy(MT[:, :, 1:2], RTs(3)),
                 reads=[rt_b], writes=[mt_b])
            s.op("dv", lambda v: v.tensor_copy(MT[:, :, 2:3], RTs(4)),
                 reads=[rt_b], writes=[mt_b])
            s.op("dv", lambda v: v.tensor_copy(
                MT[:, :, 3:4], tok_sb[:].rearrange("p (t o) -> p t o", o=1)),
                reads=[consts_b], writes=[mt_b])

            for t in range(NT):
                r = t % 2
                pay_b[r].rotate()
                s.op("dv", lambda v, r=r: v.tensor_copy(
                    pay[:, r, 0:1024], xrel[:, r, :]),
                    reads=[xrel_b[r]], writes=[pay_b[r]])
                s.op("dv", lambda v, r=r, t=t: v.tensor_copy(
                    pay[:, r, 1024:1028], MT[:, t, :]),
                    reads=[mt_b], writes=[pay_b[r]])
                for j in range(2):
                    scX_toks.append(s.op("gp", lambda g, r=r, t=t, j=j:
                                         g.indirect_dma_start(
                                             xg_d,
                                             IndirectOffsetOnAxis(IX[:, t, j:j + 1], 0),
                                             pay[:, r, :], None,
                                             bounds_check=bc_xg, oob_is_err=False),
                                         reads=[pay_b[r], ix_b],
                                         waits=[zf_toks[-1]], chan="scX%d" % r))
                wbf_b.rotate()
                s.op("dv", lambda v, t=t: v.tensor_copy(wbf[:], w_sb[:, t, :]),
                     reads=w_b, writes=[wbf_b])
                pwt_b.rotate()
                s.group("pe", [lambda t_: t_.transpose(PwT[:], wbf[:], id_sb[:])],
                        reads=[wbf_b, consts_b], writes=[pwt_b])
                wT_b[r].rotate()
                s.op("dv", lambda v, r=r: v.tensor_copy(wT[0:8, r, :], PwT[:]),
                     reads=[pwt_b], writes=[wT_b[r]])
                pb2_b.rotate()
                s.group("pe", [
                    (lambda t_, c=c, r=r: t_.matmul(
                        Pb2[:, 512 * c:512 * (c + 1)], wT[0:8, r, :],
                        b2sb[0:8, 512 * c:512 * (c + 1)], start=True, stop=True))
                    for c in range(2)],
                    reads=[wT_b[r], consts_b], writes=[pb2_b])
                o_b[r].rotate()
                for c in range(2):
                    s.op("dv", lambda v, c=c, rr=r: v.tensor_tensor(
                        o_sb[:, rr, 512 * c:512 * (c + 1)],
                        Pb2[:, 512 * c:512 * (c + 1)],
                        xrel[:, rr, 512 * c:512 * (c + 1)], ALU.add),
                        reads=[pb2_b, xrel_b[r]], writes=[o_b[r]])
                aci_toks.append(s.op("sy", lambda g, t=t, rr=r: g.dma_start(
                    acc_d[128 * t:128 * (t + 1), :], o_sb[:, rr, :]),
                    reads=[o_b[r]], chan="aci%d" % r))
                if t + 2 < NT:
                    xrel_b[r].rotate()
                    s.op("sy", lambda g, t=t, r=r: g.dma_start(
                        xrel[:, r, :], x1sp_d[128 * (t + 2):128 * (t + 3), :]),
                        writes=[xrel_b[r]], waits=[x1sp_tok[t + 2]],
                        chan="xr%d" % r)

        s.barrier(("pe", "dv", "ac"))

        Ph = es.enter_context(nc.psum_tensor("Ph", [128, 2, 1024], F32))
        Py = es.enter_context(nc.psum_tensor("Py", [128, 2, 1024], F32))
        ph_b = [B(), B()]
        py_b = [B(), B()]

        def w1_load(e, fc):
            wsl = fc % 3
            w1_b[wsl].rotate()
            s.op("gp", lambda gg, e=e, fc=fc, wsl=wsl: gg.dma_start(
                W1t[:, wsl, :, :], W1r_d[e, fc]),
                writes=[w1_b[wsl]], chan="w1%d" % wsl)

        def xga_load(e, gt):
            sl = gt % 2
            xga_b[sl].rotate()
            s.op("gp", lambda g, e=e, gt=gt, sl=sl: g.dma_start(
                xga[:, sl, :],
                xg_d[e * CAP + 128 * gt:e * CAP + 128 * (gt + 1), :]),
                writes=[xga_b[sl]], waits=scX_toks[-2:],
                chan="xga%d" % sl)

        for e in range(E):
            cur = []
            xgT_b.rotate(); wgc_b.rotate(); idxg_b.rotate()
            for gt in range(GT):
                sl = gt % 2
                if e == 0 and gt < 2:
                    xga_load(e, gt)
                s.op("dv", lambda v, gt=gt, sl=sl, e=e: v.tensor_scalar(
                    wgc[:, gt:gt + 1], xga[:, sl, 1026:1027], float(e), None,
                    ALU.is_equal),
                    reads=[xga_b[sl]], writes=[wgc_b])
                s.op("dv", lambda v, gt=gt, sl=sl: v.tensor_tensor(
                    dscr[:, 0:1], xga[:, sl, 1024:1025],
                    xga[:, sl, 1025:1026], ALU.subtract),
                    reads=[xga_b[sl]], writes=[dsc_b])
                s.op("dv", lambda v, gt=gt, sl=sl: v.tensor_tensor(
                    wgc[:, gt:gt + 1], wgc[:, gt:gt + 1],
                    dscr[:, 0:1], ALU.mult),
                    reads=[wgc_b, dsc_b], writes=[wgc_b])
                s.op("dv", lambda v, gt=gt, sl=sl: v.tensor_tensor(
                    wgc[:, gt:gt + 1], wgc[:, gt:gt + 1],
                    xga[:, sl, 1025:1026], ALU.add),
                    reads=[wgc_b, xga_b[sl]], writes=[wgc_b])
                s.op("dv", lambda v, gt=gt, sl=sl: v.tensor_copy(
                    idxg[:, gt:gt + 1], xga[:, sl, 1027:1028]),
                    reads=[xga_b[sl]], writes=[idxg_b])
                pp = gt % 2
                ph_b[pp].rotate()
                s.group("pe", [
                    (lambda t_, j=j, sl=sl, pp=pp: t_.transpose(
                        Ph[:, pp, 128 * j:128 * (j + 1)],
                        xga[:, sl, 128 * j:128 * (j + 1)], id32[:]))
                    for j in range(8)],
                    reads=[xga_b[sl], consts_b], writes=[ph_b[pp]])
                for j in range(8):
                    s.op("dv", lambda v, j=j, gt=gt, pp=pp: v.tensor_copy(
                        xgT[:, j, 128 * gt:128 * (gt + 1)],
                        Ph[:, pp, 128 * j:128 * (j + 1)]),
                        reads=[ph_b[pp]], writes=[xgT_b])
                if gt + 2 < GT:
                    xga_load(e, gt + 2)
            for g in range(NG):
                for fg in range(3):
                    if g == 0 and e > 0:
                        break
                    w1_load(e, FCG * g + fg)
                w2_b.rotate()
                for fg in range(FCG):
                    wsl2 = fg % 2
                    xrel_b[wsl2].rotate()
                    s.op("sy", lambda gg, e=e, g=g, fg=fg, wsl2=wsl2: gg.dma_start(
                        xrel[:, wsl2, :],
                        W2_d[e, 128 * (FCG * g + fg):128 * (FCG * g + fg) + 128, :]),
                        writes=[xrel_b[wsl2]], chan="w2f%d" % wsl2)
                    s.op("dv", lambda v, fg=fg, wsl2=wsl2: v.tensor_copy(
                        W2g[:, fg, :], xrel[:, wsl2, :]),
                        reads=[xrel_b[wsl2]], writes=[w2_b])
                for fg in range(FCG):
                    fc = FCG * g + fg
                    wsl = fc % 3
                    hT_b[fg].rotate()
                    psl = fg % 2
                    ph_b[psl].rotate()
                    fns = []
                    for k in range(KC):
                        for c in range(2):
                            fns.append(lambda t_, k=k, c=c, psl=psl, wsl=wsl: t_.matmul(
                                Ph[:, psl, 512 * c:512 * (c + 1)],
                                W1t[:, wsl, k, :],
                                xgT[:, k, 512 * c:512 * (c + 1)],
                                start=(k == 0), stop=(k == KC - 1)))
                    s.group("pe", fns, reads=[w1_b[wsl], xgT_b],
                            writes=[ph_b[psl]])
                    if fg + 3 < FCG:
                        w1_load(e, FCG * g + fg + 3)
                    for c in range(2):
                        s.op("ac", lambda a, fg=fg, c=c, psl=psl, e=e,
                             fc=fc: a.activation(
                                 hT[:, fg, 512 * c:512 * (c + 1)],
                                 Ph[:, psl, 512 * c:512 * (c + 1)],
                                 AF.Gelu_apprx_tanh, bias=b1c[:, e, fc:fc + 1]),
                             reads=[ph_b[psl], consts_b], writes=[hT_b[fg]])
                if g == NG - 1 and e + 1 < E:
                    xga_load(e + 1, 0)
                    xga_load(e + 1, 1)
                    for fg in range(3):
                        w1_load(e + 1, fg)
                for gt in range(GT):
                    psl = gt % 2
                    py_b[psl].rotate()
                    fns = []
                    for fg in range(FCG):
                        for c in range(2):
                            fns.append(lambda t_, fg=fg, c=c, gt=gt, psl=psl: t_.matmul(
                                Py[:, psl, 512 * c:512 * (c + 1)],
                                hT[:, fg, 128 * gt:128 * (gt + 1)],
                                W2g[:, fg, 512 * c:512 * (c + 1)],
                                start=(fg == 0), stop=(fg == FCG - 1)))
                    s.group("pe", fns,
                            reads=[w2_b] + [hT_b[q] for q in range(FCG)],
                            writes=[py_b[psl]])
                    if g == 0:
                        accg_b[gt].rotate()
                        for c in range(2):
                            s.op("dv", lambda v, gt=gt, c=c, psl=psl:
                                 v.tensor_single_scalar(
                                     accg[:, gt, 512 * c:512 * (c + 1)],
                                     Py[:, psl, 512 * c:512 * (c + 1)],
                                     wgc[:, gt:gt + 1], ALU.mult),
                                 reads=[py_b[psl], wgc_b], writes=[accg_b[gt]])
                    else:
                        for c in range(2):
                            s.op("dv", lambda v, gt=gt, c=c, psl=psl:
                                 v.scalar_tensor_tensor(
                                     accg[:, gt, 512 * c:512 * (c + 1)],
                                     Py[:, psl, 512 * c:512 * (c + 1)],
                                     wgc[:, gt:gt + 1],
                                     accg[:, gt, 512 * c:512 * (c + 1)],
                                     ALU.mult, ALU.add),
                                 reads=[py_b[psl], accg_b[gt], wgc_b],
                                 writes=[accg_b[gt]])
                        prev = (list(scA_by_e[-1]) if scA_by_e
                                else list(aci_toks[-2:]))
                        cur.append(s.op("gp", lambda g_, e=e, gt=gt:
                                        g_.indirect_dma_start(
                                            acc_d,
                                            IndirectOffsetOnAxis(idxg[:, gt:gt + 1], 0),
                                            accg[:, gt, :], None,
                                            bounds_check=bc_ac, oob_is_err=False,
                                            compute_op=ALU.add),
                                        reads=[accg_b[gt], idxg_b],
                                        waits=prev + aci_toks[-2:],
                                        chan="scA%d" % gt))
            scA_by_e.append(cur)

        out_toks = []
        for t in range(NT):
            r = t % 2
            xrel_b[r].rotate()
            s.op("sy", lambda g, t=t, rr=r: g.dma_start(
                xrel[:, rr, :], acc_d[128 * t:128 * (t + 1), :]),
                writes=[xrel_b[r]], waits=scA_by_e[-1], chan="xr%d" % r)
            ssl, bsl = stat_slot()
            stx_b[ssl].rotate(); bnx_b[bsl].rotate()
            s.op("dv", lambda v, bsl=bsl, rr=r: v.bn_stats(
                bnx[:, bsl, 0:6], xrel[:, rr, 0:512]),
                reads=[xrel_b[r]], writes=[bnx_b[bsl]])
            s.op("dv", lambda v, bsl=bsl, rr=r: v.bn_stats(
                bnx[:, bsl, 6:12], xrel[:, rr, 512:1024]),
                reads=[xrel_b[r]], writes=[bnx_b[bsl]])
            s.op("dv", lambda v, ssl=ssl, bsl=bsl: v.bn_aggr(
                stx[:, ssl, 0:2], bnx[:, bsl, 0:12]),
                reads=[bnx_b[bsl]], writes=[stx_b[ssl]])
            s.op("ac", lambda a, ssl=ssl: a.activation(
                stx[:, ssl, 2:3], stx[:, ssl, 1:2], AF.Sqrt, bias=eps[:, 0:1]),
                reads=[stx_b[ssl], consts_b], writes=[stx_b[ssl]])
            s.op("dv", lambda v, ssl=ssl: v.reciprocal(
                stx[:, ssl, 3:4], stx[:, ssl, 2:3]),
                reads=[stx_b[ssl]], writes=[stx_b[ssl]])
            o_b[r].rotate()
            s.op("dv", lambda v, rr=r, ssl=ssl: v.tensor_scalar(
                o_sb[:, rr, :], xrel[:, rr, :], stx[:, ssl, 0:1], stx[:, ssl, 3:4],
                ALU.subtract, ALU.mult),
                reads=[xrel_b[r], stx_b[ssl]], writes=[o_b[r]])
            out_toks.append(s.op("sy", lambda g, t=t, rr=r: g.dma_start(
                out_d[128 * t:128 * (t + 1), :], o_sb[:, rr, :]),
                reads=[o_b[r]], chan="o%d" % r))

        s.ops.append(("sy", lambda g: g.nop(), out_toks[-2:], None))

        semnames = {sem for _, _, waits, inc in s.ops
                    for sem, _v in (list(waits) + ([inc[0:1][0]] if False else []))}
        semnames |= {inc[0] for _, _, _, inc in s.ops if inc is not None}
        sems = {name: es.enter_context(nc.semaphore(name))
                for name in sorted(semnames)}
        block = es.enter_context(nc.Block())
        emit(block, s, sems)

    return nc


def _prep(inputs):
    def f32(a):
        return np.ascontiguousarray(np.asarray(a), dtype=np.float32)

    x = f32(inputs["L0_attn_x"])
    shared = {
        "Wq": f32(inputs["L0_attn_Wq"]), "Wk": f32(inputs["L0_attn_Wk"]),
        "Wv": f32(inputs["L0_attn_Wv"]), "Wg": f32(inputs["L0_attn_Wg"]),
        "Wo": f32(inputs["L0_attn_Wo"]),
        "bq": f32(inputs["L0_attn_bq"]).reshape(1, D),
        "bk": f32(inputs["L0_attn_bk"]).reshape(1, D),
        "bv": f32(inputs["L0_attn_bv"]).reshape(1, D),
        "bo": f32(inputs["L0_attn_bo"]).reshape(1, D),
        "gam": np.ascontiguousarray(f32(inputs["L0_attn_gamma"]).reshape(8, 128).T),
        "bet": np.ascontiguousarray(f32(inputs["L0_attn_beta"]).reshape(8, 128).T),
        "bgc": np.ascontiguousarray(f32(inputs["L0_attn_bg"]).reshape(8, 128).T),
        "rw": np.ascontiguousarray(
            f32(inputs["L0_ffn_router_w"]).T.reshape(8, 128, 8).transpose(1, 0, 2)),
        "b1c": np.ascontiguousarray(
            f32(inputs["L0_ffn_b1"]).reshape(8, 32, 128).transpose(2, 0, 1)),
        "b2": f32(inputs["L0_ffn_b2"]),
        "W1r": np.ascontiguousarray(
            f32(inputs["L0_ffn_W1"]).reshape(E, KC, 128, 32, 128)
            .transpose(0, 3, 2, 1, 4)),
        "W2": f32(inputs["L0_ffn_W2"]),
        "ident": np.eye(128, dtype=np.float32),
        "trils": np.triu(np.ones((128, 128), np.float32), k=1),
        "onesf": np.ones((128, 128), np.float32),
        "ce": np.broadcast_to(
            (np.arange(8, dtype=np.float32) * CAP)[None, None, :],
            (128, 16, 8)).copy(),
        "tokid": (np.arange(16, dtype=np.float32)[None, :] * 128
                  + np.arange(128, dtype=np.float32)[:, None]).copy(),
    }
    ii = np.arange(128, dtype=np.float64)[:, None]
    tt = np.arange(16, dtype=np.float64)[None, :]
    gidx = 128.0 * tt + ii
    shared["causal"] = (np.arange(128)[None, :] >= np.arange(128)[:, None]
                        ).astype(np.float32)
    shared["lamq"] = (LAM ** gidx / 8.0).astype(np.float32)
    shared["lamk"] = (LAM ** (-gidx)).astype(np.float32)
    return x, shared


def kernel(**inputs):
    x, shared = _prep(inputs)
    nc = build_nc()
    in_maps = [{"x": np.ascontiguousarray(x[c]), **shared} for c in range(NCORE)]
    res = run_bass_kernel_spmd(nc, in_maps, core_ids=list(range(NCORE)))
    return np.stack([res.results[c]["out"] for c in range(NCORE)], axis=0)


# version 19 (final)
# speedup vs baseline: 1.0216x; 1.0216x over previous
import contextlib
import numpy as np
import concourse.bass as bass
import concourse.mybir as mybir
from concourse.bass_core_rust import IndirectOffsetOnAxis
from concourse.bass_utils import run_bass_kernel_spmd

F32 = mybir.dt.float32
BF16 = mybir.dt.bfloat16
AF = mybir.ActivationFunctionType
ALU = mybir.AluOpType
AX = mybir.AxisListType

S, D, H, DH, E, FF = 2048, 1024, 16, 64, 8, 4096
NT = 16
KC = 8
LAM = 0.99
NCORE = 8
NG = 2
FCG = 16
CAP = 1024
GT = CAP // 128
BIGF = 1.0e9

ESEM = {"pe": "pe", "dv": "dv", "ac": "ac", "sy": "qX", "gp": "qW"}


class Buf:
    __slots__ = ("w", "r", "war")

    def __init__(self):
        self.w, self.r, self.war = [], [], []

    def rotate(self):
        self.war = list({t for t in self.w + self.r + self.war})
        self.w, self.r = [], []

    def inherit(self, others):
        for o in others:
            self.war += o.w + o.r + o.war


class Sched:
    def __init__(self):
        self.ops = []
        self.cnt = {}
        self.pending = {}

    def _tok(self, sem, n):
        c = self.cnt.get(sem, 0) + n
        self.cnt[sem] = c
        return (sem, c)

    def op(self, eng, fn, reads=(), writes=(), waits=(), n=None, chan=None):
        sem = chan or ESEM[eng]
        if n is None:
            n = 16 if eng in ("sy", "gp") else 1
        wl = list(waits)
        for b in reads:
            wl += b.w
        for b in writes:
            wl += b.war + b.r
        wl += self.pending.pop(eng, [])
        tok = self._tok(sem, n)
        self.ops.append((eng, fn, wl, (sem, n)))
        for b in reads:
            b.r.append(tok)
        for b in writes:
            b.w.append(tok)
        return tok

    def group(self, eng, fns, reads=(), writes=(), waits=()):
        sem = ESEM[eng]
        wl = list(waits)
        for b in reads:
            wl += b.w
        for b in writes:
            wl += b.war + b.r
        wl += self.pending.pop(eng, [])
        tok = self._tok(sem, 1)
        nf = len(fns)
        for i, fn in enumerate(fns):
            self.ops.append((eng, fn, wl if i == 0 else [],
                             (sem, 1) if i == nf - 1 else None))
        for b in reads:
            b.r.append(tok)
        for b in writes:
            b.w.append(tok)
        return tok

    def barrier(self, engines=("pe", "dv", "ac", "sy", "gp")):
        snap = list(self.cnt.items())
        for e in engines:
            self.pending.setdefault(e, []).extend(snap)


def emit(block, sched, sems):
    streams = {e: [] for e in ("pe", "dv", "ac", "sy", "gp")}
    for rec in sched.ops:
        streams[rec[0]].append(rec)
    for e, wl in list(sched.pending.items()):
        if wl:
            streams[e].append((e, lambda eng: eng.nop(), wl, None))
    sched.pending = {}

    def run(eng_obj, recs):
        maxw = {}
        for _, fn, waits, inc in recs:
            need = {}
            for sname, v in waits:
                if need.get(sname, 0) < v:
                    need[sname] = v
            for sname, v in need.items():
                if maxw.get(sname, 0) < v:
                    maxw[sname] = v
                    eng_obj.wait_ge(sems[sname], v)
            inst = fn(eng_obj)
            if inc is not None:
                inst.then_inc(sems[inc[0]], inc[1])

    block.tensor(lambda t: run(t, streams["pe"]))
    block.vector(lambda v: run(v, streams["dv"]))
    block.scalar(lambda a: run(a, streams["ac"]))
    block.sync(lambda sy: run(sy, streams["sy"]))
    block.gpsimd(lambda g: run(g, streams["gp"]))


def build_nc(dbg=()):
    nc = bass.Bass("TRN2", target_bir_lowering=False)

    def dt(name, shape, dtype=F32, kind="ExternalInput"):
        return nc.dram_tensor(name, shape, dtype, kind=kind).ap()

    x_d = dt("x", [S, D])
    Wq_d = dt("Wq", [D, D]); Wk_d = dt("Wk", [D, D]); Wv_d = dt("Wv", [D, D])
    Wg_d = dt("Wg", [D, D]); Wo_d = dt("Wo", [D, D])
    bq_d = dt("bq", [1, D]); bk_d = dt("bk", [1, D])
    bv_d = dt("bv", [1, D]); bo_d = dt("bo", [1, D])
    gam_d = dt("gam", [128, 8]); bet_d = dt("bet", [128, 8]); bgc_d = dt("bgc", [128, 8])
    rw_d = dt("rw", [128, 8, 8])
    b1c_d = dt("b1c", [128, 8, 32])
    b2_d = dt("b2", [8, D])
    W1r_d = dt("W1r", [E, 32, 128, 8, 128])
    W2_d = dt("W2", [E, FF, D])
    id_d = dt("ident", [128, 128])
    tri_d = dt("trils", [128, 128])
    on_d = dt("onesf", [128, 128])
    ce_d = dt("ce", [128, 16, 8])
    tok_d = dt("tokid", [128, 16])
    xg_d = dt("xg", [E * CAP, 1028], F32, "Internal")
    acc_d = dt("accd", [S, D], F32, "Internal")
    ca_d = dt("causal", [128, 128])
    lamq_d = dt("lamq", [128, 16])
    lamk_d = dt("lamk", [128, 16])
    out_d = dt("out", [S, D], F32, "ExternalOutput")
    x1sp_d = dt("x1sp", [S, D], F32, "Internal")
    dbg_d = {name: dt("dbg_" + name, [S, D], F32, "ExternalOutput") for name in dbg}

    s = Sched()
    B = Buf

    with contextlib.ExitStack() as es:
        def sb(name, shape, dtype, side="left"):
            return es.enter_context(nc.sbuf_tensor(name, shape, dtype, side=side))

        id_sb = sb("id_sb", [128, 128], BF16)
        ca_sb = sb("ca_sb", [128, 128], BF16)
        ones = sb("ones", [1, 128], BF16)
        lamq = sb("lamq_sb", [128, 16], F32)
        lamk = sb("lamk_sb", [128, 16], F32)
        eps = sb("eps_sb", [128, 1], F32)
        gam = sb("gam_sb", [128, 8], F32)
        bet = sb("bet_sb", [128, 8], F32)
        bgc = sb("bgc_sb", [128, 8], F32)
        bq_r = sb("bq_r", [1, D], BF16)
        bk_r = sb("bk_r", [1, D], BF16)
        bv_r = sb("bv_r", [1, D], BF16)
        bo_r = sb("bo_r", [1, D], BF16)
        b1c = sb("b1c_sb", [128, 8, 32], F32)
        b2sb = sb("b2_sb", [8, D], BF16)
        rw = sb("rw_sb", [128, 8, 8], BF16)
        Ssb = sb("Ssb", [64, 1024], BF16)
        x1T = sb("x1T", [128, 8, S], BF16)
        w_sb = sb("w_sb", [128, 16, 8], F32)
        stx = sb("stx", [128, 6, 4], F32)
        bnx = sb("bnx", [128, 4, 12], F32)
        tri_sb = sb("tri_sb", [128, 128], BF16)
        id32 = sb("id32", [128, 128], F32)
        on_sb = sb("on_sb", [128, 128], BF16)
        ce16 = sb("ce16", [128, 16, 8], F32)
        tok_sb = sb("tok_sb", [128, 16], F32)
        zrow = sb("zrow", [128, 1028], F32)
        flg = sb("flg", [128, 16, 8], BF16)

        bc_xg = es.enter_context(nc.gpsimd.register("bc_xg"))
        bc_ac = es.enter_context(nc.gpsimd.register("bc_ac"))

        consts_b = B()
        ssb_b = B()
        x1T_b = B()
        w_b = [B() for _ in range(NT)]
        stx_b = [B() for _ in range(6)]
        bnx_b = [B() for _ in range(4)]
        _stc = [0]

        def stat_slot():
            sl, bl = _stc[0] % 6, _stc[0] % 4
            _stc[0] += 1
            return sl, bl

        for dst_t, src in ((id_sb, id_d), (ca_sb, ca_d), (b2sb, b2_d),
                           (tri_sb, tri_d), (on_sb, on_d),
                           (bq_r, bq_d), (bk_r, bk_d), (bv_r, bv_d), (bo_r, bo_d),
                           (rw, rw_d)):
            s.op("gp", lambda g, a=dst_t, bb=src: g.dma_start(a[:], bb),
                 writes=[consts_b], chan="cG")
        for dst_t, src in ((lamq, lamq_d), (lamk, lamk_d), (gam, gam_d),
                           (bet, bet_d), (bgc, bgc_d), (b1c, b1c_d),
                           (ce16, ce_d), (tok_sb, tok_d), (id32, id_d)):
            s.op("sy", lambda g, a=dst_t, bb=src: g.dma_start(a[:], bb),
                 writes=[consts_b], chan="cS")
        s.op("gp", lambda g: g.reg_mov(bc_xg, E * CAP - 1), chan="gpr", n=1)
        s.op("gp", lambda g: g.reg_mov(bc_ac, S - 1), chan="gpr", n=1)
        s.op("dv", lambda v: v.memset(ones[:], 1.0), writes=[consts_b])
        s.op("dv", lambda v: v.memset(eps[:], 1e-5), writes=[consts_b])
        z0_tok = s.op("dv", lambda v: v.memset(zrow[:], 0.0), writes=[consts_b])
        s.op("dv", lambda v: v.memset(
            zrow[:, 1027:1028], 2048.0), writes=[consts_b],
            waits=[z0_tok])
        zf_toks = []

        with contextlib.ExitStack() as esA:
            def sbA(name, shape, dtype):
                return esA.enter_context(nc.sbuf_tensor(name, shape, dtype, side="right"))

            xtile = sbA("xtile", [128, 2, 1024], F32)
            xbf = sbA("xbf", [128, 1024], BF16)
            xTh = sbA("xTh", [128, 8, 1024], BF16)
            Vtok = sbA("Vtok", [128, 8, 1024], BF16)
            Qtok = sbA("Qtok", [128, 8, 1024], BF16)
            Ktok = sbA("Ktok", [128, 8, 1024], BF16)
            Gt = sbA("Gt", [128, 8, 1024], BF16)
            Wa = sbA("Wa", [128, 8, 1024], BF16)
            Wb = sbA("Wb", [128, 8, 1024], BF16)
            qf = sbA("qf", [64, 16, 128], BF16)
            kf = sbA("kf", [64, 16, 128], BF16)
            at = sbA("at", [128, 16, 128], BF16)
            rn = sbA("rn", [128, 1024], BF16)
            rnT = sbA("rnT", [128, 8, 128], BF16)
            ngT = sbA("ngT", [128, 8, 128], BF16)
            sum_sb = sbA("sum_sb", [128, 1024], F32)
            x1f = sbA("x1f", [128, 1024], F32)
            x1b = sbA("x1b", [128, 1024], BF16)

            STp = esA.enter_context(nc.psum_tensor("STp", [64, 1024], F32))
            Tp = esA.enter_context(nc.psum_tensor("Tp", [128, 2, 8, 128], BF16))
            PP = esA.enter_context(nc.psum_tensor("PP", [128, 2, 1024], F32))

            def SCap(sl):
                return PP[:, 0, 128 * sl:128 * (sl + 1)]

            RWap = PP[:, 1, :]

            xt_b = [B(), B()]
            xbf_b = B()
            xTh_b = [B() for _ in range(8)]
            vt_b = [B() for _ in range(8)]
            qt_b = [B() for _ in range(8)]
            kt_b = [B() for _ in range(8)]
            gt_b = [B() for _ in range(8)]
            wa_b, wb_b = B(), B()
            qf_b, kf_b, at_b = B(), B(), B()
            rn_b, rnT_b, ngT_b = B(), B(), B()
            sum_b, x1f_b, x1b_b = B(), B(), B()
            st_b = B()
            tp_b = [B(), B()]
            pp_b = [B(), B()]
            sc_b = [B(), B()]
            rw_bk = [B(), B()]
            x1sp_tok = [None] * NT

            for hf in range(2):
                wa_b.rotate(); wb_b.rotate()
                for k in range(KC):
                    s.op("gp", lambda g, k=k: g.dma_start(
                        Wa[:, k, :], Wv_d[128 * k:128 * (k + 1), :]),
                        writes=[wa_b], chan="wV")
                for k in range(KC):
                    s.op("gp", lambda g, k=k: g.dma_start(
                        Wb[:, k, :], Wg_d[128 * k:128 * (k + 1), :]),
                        writes=[wb_b], chan="wG")

                for tt in range(8):
                    t = 8 * hf + tt
                    sl = tt % 2
                    xt_b[sl].rotate()
                    s.op("sy", lambda g, t=t, sl=sl: g.dma_start(
                        xtile[:, sl, :], x_d[128 * t:128 * (t + 1), :]),
                        writes=[xt_b[sl]], chan="xt%d" % sl)
                    xbf_b.rotate()
                    s.op("dv", lambda v, sl=sl: v.tensor_copy(xbf[:], xtile[:, sl, :]),
                         reads=[xt_b[sl]], writes=[xbf_b])
                    pb = tp_b[tt % 2]
                    pb.rotate()
                    s.group("pe", [
                        (lambda t_, j=j, pp=tt % 2: t_.transpose(
                            Tp[:, pp, j, :], xbf[:, 128 * j:128 * (j + 1)], id_sb[:]))
                        for j in range(8)],
                        reads=[xbf_b, consts_b], writes=[pb])
                    xTh_b[tt].rotate()
                    for j in range(8):
                        s.op("ac", lambda a, j=j, tt=tt, pp=tt % 2: a.copy(
                            xTh[:, j, 128 * tt:128 * (tt + 1)], Tp[:, pp, j, :]),
                            reads=[pb], writes=[xTh_b[tt]])

                for tt in range(8):
                    sl = tt % 2
                    pp_b[sl].rotate()
                    fns = [
                        (lambda t_, sl=sl: t_.matmul(
                            PP[:, sl, 0:512], ones[:], bv_r[0:1, 0:512],
                            start=True, stop=False)),
                        (lambda t_, sl=sl: t_.matmul(
                            PP[:, sl, 512:1024], ones[:], bv_r[0:1, 512:1024],
                            start=True, stop=False)),
                    ]
                    for k in range(KC):
                        for c in range(2):
                            fns.append(lambda t_, k=k, c=c, sl=sl, tt=tt: t_.matmul(
                                PP[:, sl, 512 * c:512 * (c + 1)],
                                xTh[:, k, 128 * tt:128 * (tt + 1)],
                                Wa[:, k, 512 * c:512 * (c + 1)],
                                start=False, stop=(k == KC - 1)))
                    s.group("pe", fns, reads=[xTh_b[tt], wa_b, consts_b],
                            writes=[pp_b[sl]])
                    vt_b[tt].rotate()
                    s.op("dv", lambda v, tt=tt, sl=sl: v.tensor_copy(
                        Vtok[:, tt, :], PP[:, sl, :]),
                        reads=[pp_b[sl]], writes=[vt_b[tt]])

                for j in range(8):
                    sl = j % 2
                    pp_b[sl].rotate()
                    fns = []
                    for k in range(KC):
                        for c in range(2):
                            fns.append(lambda t_, k=k, c=c, sl=sl, j=j: t_.matmul(
                                PP[:, sl, 512 * c:512 * (c + 1)],
                                Wb[:, k, 128 * j:128 * (j + 1)],
                                xTh[:, k, 512 * c:512 * (c + 1)],
                                start=(k == 0), stop=(k == KC - 1)))
                    s.group("pe", fns,
                            reads=[wb_b, consts_b] + [xTh_b[q] for q in range(8)],
                            writes=[pp_b[sl]])
                    gt_b[j].rotate()
                    for c in range(2):
                        s.op("ac", lambda a, j=j, c=c, sl=sl: a.activation(
                            Gt[:, j, 512 * c:512 * (c + 1)],
                            PP[:, sl, 512 * c:512 * (c + 1)],
                            AF.Sigmoid, bias=bgc[:, j:j + 1]),
                            reads=[pp_b[sl], consts_b], writes=[gt_b[j]])

                wa_b.rotate()
                for k in range(KC):
                    s.op("gp", lambda g, k=k: g.dma_start(
                        Wa[:, k, :], Wq_d[128 * k:128 * (k + 1), :]),
                        writes=[wa_b], chan="wQ")
                wb_b.rotate()
                for k in range(KC):
                    s.op("gp", lambda g, k=k: g.dma_start(
                        Wb[:, k, :], Wk_d[128 * k:128 * (k + 1), :]),
                        writes=[wb_b], chan="wK")

                for which in ("q", "k"):
                    if which == "q":
                        warena, wt, brow, lamv, dst, dst_b = wa_b, Wa, bq_r, lamq, Qtok, qt_b
                    else:
                        warena, wt, brow, lamv, dst, dst_b = wb_b, Wb, bk_r, lamk, Ktok, kt_b
                    for tt in range(8):
                        t = 8 * hf + tt
                        sl = tt % 2
                        pp_b[sl].rotate()
                        fns = [
                            (lambda t_, sl=sl, brow=brow: t_.matmul(
                                PP[:, sl, 0:512], ones[:], brow[0:1, 0:512],
                                start=True, stop=False)),
                            (lambda t_, sl=sl, brow=brow: t_.matmul(
                                PP[:, sl, 512:1024], ones[:], brow[0:1, 512:1024],
                                start=True, stop=False)),
                        ]
                        for k in range(KC):
                            for c in range(2):
                                fns.append(lambda t_, k=k, c=c, sl=sl, tt=tt, wt=wt: t_.matmul(
                                    PP[:, sl, 512 * c:512 * (c + 1)],
                                    xTh[:, k, 128 * tt:128 * (tt + 1)],
                                    wt[:, k, 512 * c:512 * (c + 1)],
                                    start=False, stop=(k == KC - 1)))
                        s.group("pe", fns, reads=[xTh_b[tt], warena, consts_b],
                                writes=[pp_b[sl]])
                        dst_b[tt].rotate()
                        for c in range(2):
                            s.op("ac", lambda a, tt=tt, c=c, sl=sl, t=t, dst=dst,
                                 lamv=lamv: a.activation(
                                     dst[:, tt, 512 * c:512 * (c + 1)],
                                     PP[:, sl, 512 * c:512 * (c + 1)],
                                     AF.Identity, scale=lamv[:, t:t + 1]),
                                 reads=[pp_b[sl], consts_b], writes=[dst_b[tt]])

                wa_b.rotate()
                for k in range(KC):
                    s.op("gp", lambda g, k=k: g.dma_start(
                        Wa[:, k, :], Wo_d[128 * k:128 * (k + 1), :]),
                        writes=[wa_b], chan="wO")

                for b_ in sc_b:
                    b_.inherit([pp_b[0]])
                for b_ in rw_bk:
                    b_.inherit([pp_b[1]])

                pend = [None]
                for tt in range(8):
                    i = 8 * hf + tt
                    t = i
                    sl = tt % 2
                    xt_b[sl].rotate()
                    xres_tok = s.op("sy", lambda g, t=t, sl=sl: g.dma_start(
                        xtile[:, sl, :], x_d[128 * t:128 * (t + 1), :]),
                        writes=[xt_b[sl]], chan="xt%d" % sl)
                    for zz in range(4):
                        zi = 4 * i + zz
                        zf_toks.append(s.op("sy", lambda g, zi=zi: g.dma_start(
                            xg_d[128 * zi:128 * (zi + 1), :], zrow[:]),
                            reads=[consts_b], chan="zf"))

                    qf_b.rotate()
                    for grp in range(2):
                        pb = tp_b[grp]
                        pb.rotate()
                        s.group("pe", [
                            (lambda t_, hh=hh, grp=grp, tt=tt: t_.transpose(
                                Tp[0:64, grp, hh, :],
                                Qtok[:, tt, 64 * (8 * grp + hh):64 * (8 * grp + hh) + 64],
                                id_sb[:]))
                            for hh in range(8)],
                            reads=[qt_b[tt], consts_b], writes=[pb])
                        for hh in range(8):
                            s.op("ac", lambda a, hh=hh, grp=grp: a.copy(
                                qf[:, 8 * grp + hh, :], Tp[0:64, grp, hh, :]),
                                reads=[pb], writes=[qf_b])
                    kf_b.rotate()
                    for grp in range(2):
                        pb = tp_b[grp]
                        pb.rotate()
                        s.group("pe", [
                            (lambda t_, hh=hh, grp=grp, tt=tt: t_.transpose(
                                Tp[0:64, grp, hh, :],
                                Ktok[:, tt, 64 * (8 * grp + hh):64 * (8 * grp + hh) + 64],
                                id_sb[:]))
                            for hh in range(8)],
                            reads=[kt_b[tt], consts_b], writes=[pb])
                        for hh in range(8):
                            s.op("ac", lambda a, hh=hh, grp=grp: a.copy(
                                kf[:, 8 * grp + hh, :], Tp[0:64, grp, hh, :]),
                                reads=[pb], writes=[kf_b])

                    st_b.rotate()
                    s.group("pe", [
                        (lambda t_, h=h, tt=tt, i=i: t_.matmul(
                            STp[0:64, 64 * h:64 * h + 64],
                            Ktok[:, tt, 64 * h:64 * h + 64],
                            Vtok[:, tt, 64 * h:64 * h + 64],
                            start=(i == 0 and h % 8 == 0),
                            stop=(i == NT - 1 and h == H - 1),
                            skip_group_check=True))
                        for h in range(H)],
                        reads=[kt_b[tt], vt_b[tt]], writes=[st_b])

                    at_b.rotate()
                    for hq in range(4):
                        bk = hq % 2
                        sc_b[bk].rotate()
                        s.group("pe", [
                            (lambda t_, h=h: t_.matmul(
                                SCap(h % 8), kf[:, h, :], qf[:, h, :],
                                start=True, stop=True))
                            for h in range(4 * hq, 4 * hq + 4)],
                            reads=[kf_b, qf_b], writes=[sc_b[bk]])
                        for h in range(4 * hq, 4 * hq + 4):
                            s.op("dv", lambda v, h=h: v.tensor_tensor(
                                at[:, h, :], SCap(h % 8), ca_sb[:], ALU.mult),
                                reads=[sc_b[bk], consts_b], writes=[at_b])

                    if pend[0] is not None:
                        pend[0]()
                        pend[0] = None
                    for b_ in rw_bk:
                        b_.rotate()
                    fns = [
                        (lambda t_, h=h, tt=tt: t_.matmul(
                            RWap[:, 64 * h:64 * h + 64], at[:, h, :],
                            Vtok[:, tt, 64 * h:64 * h + 64],
                            start=(h % 8 == 0), stop=False, skip_group_check=True))
                        for h in range(H)]
                    if i > 0:
                        fns += [
                            (lambda t_, h=h: t_.matmul(
                                RWap[:, 64 * h:64 * h + 64], qf[:, h, :],
                                Ssb[:, 64 * h:64 * h + 64],
                                start=False, stop=(h == H - 1), skip_group_check=True))
                            for h in range(H)]
                    s.group("pe", fns, reads=[at_b, vt_b[tt], qf_b, ssb_b],
                            writes=[rw_bk[0], rw_bk[1]])

                    ssb_b.rotate()
                    s.op("ac", lambda a: a.copy(Ssb[:], STp[0:64, :]),
                         reads=[st_b], writes=[ssb_b])

                    ssl, bsl = stat_slot()
                    stx_b[ssl].rotate(); bnx_b[bsl].rotate()
                    s.op("dv", lambda v, bsl=bsl: v.bn_stats(
                        bnx[:, bsl, 0:6], RWap[:, 0:512]),
                        reads=[rw_bk[0]], writes=[bnx_b[bsl]])
                    s.op("dv", lambda v, bsl=bsl: v.bn_stats(
                        bnx[:, bsl, 6:12], RWap[:, 512:1024]),
                        reads=[rw_bk[1]], writes=[bnx_b[bsl]])
                    s.op("dv", lambda v, ssl=ssl, bsl=bsl: v.bn_aggr(
                        stx[:, ssl, 0:2], bnx[:, bsl, 0:12]),
                        reads=[bnx_b[bsl]], writes=[stx_b[ssl]])
                    s.op("ac", lambda a, ssl=ssl: a.activation(
                        stx[:, ssl, 2:3], stx[:, ssl, 1:2], AF.Sqrt, bias=eps[:, 0:1]),
                        reads=[stx_b[ssl], consts_b], writes=[stx_b[ssl]])
                    s.op("dv", lambda v, ssl=ssl: v.reciprocal(
                        stx[:, ssl, 3:4], stx[:, ssl, 2:3]),
                        reads=[stx_b[ssl]], writes=[stx_b[ssl]])
                    rn_b.rotate()
                    s.op("dv", lambda v, ssl=ssl: v.tensor_scalar(
                        rn[:], RWap[:, :], stx[:, ssl, 0:1], stx[:, ssl, 3:4],
                        ALU.subtract, ALU.mult),
                        reads=[rw_bk[0], rw_bk[1], stx_b[ssl]], writes=[rn_b])

                    def mk_tail(tt=tt, i=i, t=t, sl=sl, xres_tok=xres_tok):
                        def _tail():
                            rnT_b.rotate()
                            for grp in range(2):
                                pb = tp_b[grp]
                                pb.rotate()
                                s.group("pe", [
                                    (lambda t_, jj=jj, grp=grp: t_.transpose(
                                        Tp[:, grp, jj, :],
                                        rn[:, 128 * (4 * grp + jj):128 * (4 * grp + jj) + 128],
                                        id_sb[:]))
                                    for jj in range(4)],
                                    reads=[rn_b, consts_b], writes=[pb])
                                for jj in range(4):
                                    j = 4 * grp + jj
                                    s.op("ac", lambda a, j=j, jj=jj, grp=grp: a.activation(
                                        rnT[:, j, :], Tp[:, grp, jj, :], AF.Identity,
                                        scale=gam[:, j:j + 1], bias=bet[:, j:j + 1]),
                                        reads=[pb, consts_b], writes=[rnT_b])
                            ngT_b.rotate()
                            for j in range(8):
                                s.op("dv", lambda v, j=j, tt=tt: v.tensor_tensor(
                                    ngT[:, j, :], rnT[:, j, :],
                                    Gt[:, j, 128 * tt:128 * (tt + 1)], ALU.mult),
                                    reads=[rnT_b, gt_b[j]], writes=[ngT_b])

                            for b_ in rw_bk:
                                b_.rotate()
                            fns = [
                                (lambda t_: t_.matmul(RWap[:, 0:512], ones[:], bo_r[0:1, 0:512],
                                                      start=True, stop=False)),
                                (lambda t_: t_.matmul(RWap[:, 512:1024], ones[:],
                                                      bo_r[0:1, 512:1024], start=True, stop=False)),
                            ]
                            for k in range(KC):
                                for c in range(2):
                                    fns.append(lambda t_, k=k, c=c: t_.matmul(
                                        RWap[:, 512 * c:512 * (c + 1)], ngT[:, k, :],
                                        Wa[:, k, 512 * c:512 * (c + 1)],
                                        start=False, stop=(k == KC - 1)))
                            s.group("pe", fns, reads=[ngT_b, wa_b, consts_b],
                                    writes=[rw_bk[0], rw_bk[1]])

                            sum_b.rotate()
                            for c in range(2):
                                s.op("dv", lambda v, c=c, sl=sl: v.tensor_tensor(
                                    sum_sb[:, 512 * c:512 * (c + 1)],
                                    RWap[:, 512 * c:512 * (c + 1)],
                                    xtile[:, sl, 512 * c:512 * (c + 1)], ALU.add),
                                    reads=[rw_bk[c], xt_b[sl]], writes=[sum_b],
                                    waits=[xres_tok])
                            ssl, bsl = stat_slot()
                            stx_b[ssl].rotate(); bnx_b[bsl].rotate()
                            s.op("dv", lambda v, bsl=bsl: v.bn_stats(
                                bnx[:, bsl, 0:6], sum_sb[:, 0:512]),
                                reads=[sum_b], writes=[bnx_b[bsl]])
                            s.op("dv", lambda v, bsl=bsl: v.bn_stats(
                                bnx[:, bsl, 6:12], sum_sb[:, 512:1024]),
                                reads=[sum_b], writes=[bnx_b[bsl]])
                            s.op("dv", lambda v, ssl=ssl, bsl=bsl: v.bn_aggr(
                                stx[:, ssl, 0:2], bnx[:, bsl, 0:12]),
                                reads=[bnx_b[bsl]], writes=[stx_b[ssl]])
                            s.op("ac", lambda a, ssl=ssl: a.activation(
                                stx[:, ssl, 2:3], stx[:, ssl, 1:2], AF.Sqrt, bias=eps[:, 0:1]),
                                reads=[stx_b[ssl], consts_b], writes=[stx_b[ssl]])
                            s.op("dv", lambda v, ssl=ssl: v.reciprocal(
                                stx[:, ssl, 3:4], stx[:, ssl, 2:3]),
                                reads=[stx_b[ssl]], writes=[stx_b[ssl]])
                            x1f_b.rotate()
                            s.op("dv", lambda v, ssl=ssl: v.tensor_scalar(
                                x1f[:], sum_sb[:], stx[:, ssl, 0:1], stx[:, ssl, 3:4],
                                ALU.subtract, ALU.mult),
                                reads=[sum_b, stx_b[ssl]], writes=[x1f_b])
                            x1sp_tok[t] = s.op("sy", lambda g, t=t: g.dma_start(
                                x1sp_d[128 * t:128 * (t + 1), :], x1f[:]),
                                reads=[x1f_b], chan="sp")
                            if "x1" in dbg_d:
                                s.op("sy", lambda g, t=t: g.dma_start(
                                    dbg_d["x1"][128 * t:128 * (t + 1), :], x1f[:]),
                                    reads=[x1f_b], chan="dbg")
                            x1b_b.rotate()
                            s.op("dv", lambda v: v.tensor_copy(x1b[:], x1f[:]),
                                 reads=[x1f_b], writes=[x1b_b])
                            for grp in range(2):
                                pb = tp_b[grp]
                                pb.rotate()
                                s.group("pe", [
                                    (lambda t_, jj=jj, grp=grp: t_.transpose(
                                        Tp[:, grp, jj, :],
                                        x1b[:, 128 * (4 * grp + jj):128 * (4 * grp + jj) + 128],
                                        id_sb[:]))
                                    for jj in range(4)],
                                    reads=[x1b_b, consts_b], writes=[pb])
                                for jj in range(4):
                                    k = 4 * grp + jj
                                    s.op("ac", lambda a, k=k, jj=jj, grp=grp, t=t: a.copy(
                                        x1T[:, k, 128 * t:128 * (t + 1)], Tp[:, grp, jj, :]),
                                        reads=[pb], writes=[x1T_b])
                        return _tail
                    pend[0] = mk_tail()
                if pend[0] is not None:
                    pend[0]()
                    pend[0] = None

        s.barrier()

        def sbF(name, shape, dtype):
            return es.enter_context(nc.sbuf_tensor(name, shape, dtype, side="right"))

        hT = sbF("hT", [128, FCG, CAP], BF16)
        W2g = sbF("W2g", [128, FCG, 1024], BF16)
        W1t = sbF("W1t", [128, 3, 8, 128], BF16)
        xga = sbF("xga", [128, 2, 1028], F32)
        xgT = sbF("xgT", [128, 8, CAP], BF16)
        accg = sbF("accg", [128, GT, 1024], F32)
        wgc = sbF("wgc", [128, GT], F32)
        idxg = sbF("idxg", [128, GT], mybir.dt.int32)
        xrel = sbF("xrel", [128, 2, 1024], F32)
        o_sb = sbF("o_sb", [128, 2, 1024], F32)
        wT = sbF("wT", [8, 2, 128], BF16)
        wbf = sbF("wbf", [128, 8], BF16)
        P8 = sbF("P8", [128, NT, 8], F32)
        T8 = sbF("T8", [128, NT, 8], F32)
        FF8 = sbF("FF8", [128, NT, 8], F32)
        D8 = sbF("D8", [128, NT, 8], F32)
        PF = sbF("PF", [128, NT, 8], F32)
        RT = sbF("RT", [128, NT, 8], F32)
        MT = sbF("MT", [128, NT, 4], F32)
        IX = sbF("IX", [128, NT, 2], mybir.dt.int32)
        dscr = sbF("dscr", [128, 4], F32)
        pay = sbF("pay", [128, 2, 1028], F32)

        hT_b = [B() for _ in range(FCG)]
        w2_b = B()
        w1_b = [B(), B(), B()]
        xga_b = [B(), B()]
        xgT_b = B()
        accg_b = [B() for _ in range(GT)]
        wgc_b, idxg_b = B(), B()
        xrel_b = [B(), B()]
        o_b = [B(), B()]
        wT_b = [B(), B()]
        wbf_b = B()
        p8_b, t8_b, ff_b, d8_b, pf_b = B(), B(), B(), B(), B()
        rt_b, mt_b, ix_b, dsc_b = B(), B(), B(), B()
        pay_b = [B(), B()]
        flg_b = B()
        scX_toks = []
        aci_toks = []
        scA_by_e = []

        with contextlib.ExitStack() as esR:
            PL = esR.enter_context(nc.psum_tensor("PL", [128, 2, 512], F32))
            Ppos = esR.enter_context(nc.psum_tensor("Ppos", [128, 2, 512], F32))
            PwT = esR.enter_context(nc.psum_tensor("PwT", [8, 128], BF16))
            Pb2 = esR.enter_context(nc.psum_tensor("Pb2", [128, 1024], F32))
            pl_b = [B(), B()]
            pp2_b = [B(), B()]
            pwt_b, pb2_b = B(), B()

            def bc8(ap):
                return ap.broadcast_to([128, NT, 8])

            for t in range(NT):
                r = t % 2
                pl_b[r].rotate()
                s.group("pe", [
                    (lambda t_, k=k, t=t, r=r: t_.matmul(
                        PL[:, r, 0:8], x1T[:, k, 128 * t:128 * (t + 1)], rw[:, k, :],
                        start=(k == 0), stop=(k == KC - 1)))
                    for k in range(KC)],
                    reads=[x1T_b, consts_b], writes=[pl_b[r]])
                s.op("dv", lambda v, t=t, r=r: v.tensor_copy(
                    P8[:, t, :], PL[:, r, 0:8]),
                    reads=[pl_b[r]], writes=[p8_b])
                if t < 2:
                    xrel_b[t].rotate()
                    s.op("sy", lambda g, t=t: g.dma_start(
                        xrel[:, t, :], x1sp_d[128 * t:128 * (t + 1), :]),
                        writes=[xrel_b[t]], waits=[x1sp_tok[t]],
                        chan="xr%d" % t)

            def dvb(fn, reads, writes):
                return s.op("dv", fn, reads=reads, writes=writes)

            RTs = lambda c: RT[:, :, c:c + 1]
            dvb(lambda v: v.reduce_max(RTs(0), P8[:], axis=AX.X), [p8_b], [rt_b])
            dvb(lambda v: v.tensor_tensor(T8[:], P8[:], bc8(RTs(0)), ALU.subtract),
                [p8_b, rt_b], [t8_b])
            s.op("ac", lambda a: a.activation(P8[:], T8[:], AF.Exp),
                 reads=[t8_b], writes=[p8_b])
            dvb(lambda v: v.reduce_sum(RTs(1), P8[:], axis=AX.X), [p8_b], [rt_b])
            dvb(lambda v: v.tensor_scalar(RTs(2), RTs(1), 1e-8, None, ALU.mult),
                [rt_b], [rt_b])
            dvb(lambda v: v.reduce_max(RTs(3), P8[:], axis=AX.X), [p8_b], [rt_b])
            dvb(lambda v: v.tensor_tensor(T8[:], P8[:], bc8(RTs(3)), ALU.is_lt),
                [p8_b, rt_b], [t8_b])
            dvb(lambda v: v.tensor_tensor(T8[:], T8[:], P8[:], ALU.mult),
                [t8_b, p8_b], [t8_b])
            dvb(lambda v: v.reduce_max(RTs(4), T8[:], axis=AX.X), [t8_b], [rt_b])
            dvb(lambda v: v.tensor_tensor(RTs(5), RTs(3), RTs(4), ALU.add),
                [rt_b], [rt_b])
            dvb(lambda v: v.tensor_tensor(RTs(6), RTs(5), RTs(2), ALU.add),
                [rt_b], [rt_b])
            dvb(lambda v: v.reciprocal(RTs(7), RTs(6)), [rt_b], [rt_b])
            dvb(lambda v: v.tensor_tensor(T8[:], P8[:], bc8(RTs(4)), ALU.is_ge),
                [p8_b, rt_b], [t8_b])
            dvb(lambda v: v.tensor_tensor(T8[:], T8[:], P8[:], ALU.mult),
                [t8_b, p8_b], [t8_b])
            for t in range(NT):
                w_b[t].rotate()
            dvb(lambda v: v.tensor_tensor(w_sb[:], T8[:], bc8(RTs(7)), ALU.mult),
                [t8_b, rt_b], w_b)
            dvb(lambda v: v.tensor_scalar(FF8[:], w_sb[:], 0.0, None, ALU.is_gt),
                w_b, [ff_b])
            flg_b.rotate()
            dvb(lambda v: v.tensor_copy(flg[:], FF8[:]), [ff_b], [flg_b])

            for t in range(NT):
                r = t % 2
                pp2_b[r].rotate()
                s.group("pe",
                        [(lambda t_, tp=tp, r=r: t_.matmul(
                            Ppos[:, r, 0:8], on_sb[:], flg[:, tp, :],
                            start=(tp == 0), stop=False, skip_group_check=True))
                         for tp in range(t)] +
                        [(lambda t_, t=t, r=r: t_.matmul(
                            Ppos[:, r, 0:8], tri_sb[:], flg[:, t, :],
                            start=(t == 0), stop=True, skip_group_check=True))],
                        reads=[flg_b, consts_b], writes=[pp2_b[r]])
                s.op("dv", lambda v, t=t, r=r: v.tensor_copy(
                    PF[:, t, :], Ppos[:, r, 0:8]),
                    reads=[pp2_b[r]], writes=[pf_b])

            dvb(lambda v: v.tensor_scalar(T8[:], PF[:], float(CAP), None, ALU.is_lt),
                [pf_b], [t8_b])
            dvb(lambda v: v.tensor_tensor(FF8[:], FF8[:], T8[:], ALU.mult),
                [ff_b, t8_b], [ff_b])
            dvb(lambda v: v.tensor_tensor(D8[:], PF[:], ce16[:], ALU.add),
                [pf_b, consts_b], [d8_b])
            dvb(lambda v: v.tensor_tensor(D8[:], D8[:], FF8[:], ALU.mult),
                [d8_b, ff_b], [d8_b])
            dvb(lambda v: v.tensor_scalar(T8[:], FF8[:], -BIGF, BIGF,
                                          ALU.mult, ALU.add),
                [ff_b], [t8_b])
            dvb(lambda v: v.tensor_tensor(D8[:], D8[:], T8[:], ALU.add),
                [d8_b, t8_b], [d8_b])
            dvb(lambda v: v.tensor_reduce(RTs(0), D8[:], axis=AX.X, op=ALU.min),
                [d8_b], [rt_b])
            dvb(lambda v: v.tensor_tensor(T8[:], D8[:], bc8(RTs(0)), ALU.is_equal),
                [d8_b, rt_b], [t8_b])
            dvb(lambda v: v.tensor_tensor(FF8[:], T8[:], ce16[:], ALU.mult),
                [t8_b, consts_b], [ff_b])
            dvb(lambda v: v.reduce_sum(RTs(4), FF8[:], axis=AX.X), [ff_b], [rt_b])
            dvb(lambda v: v.tensor_scalar(RTs(4), RTs(4), 1.0 / CAP, None, ALU.mult),
                [rt_b], [rt_b])
            dvb(lambda v: v.tensor_tensor(FF8[:], T8[:], w_sb[:], ALU.mult),
                [t8_b] + w_b, [ff_b])
            dvb(lambda v: v.reduce_sum(RTs(1), FF8[:], axis=AX.X), [ff_b], [rt_b])
            dvb(lambda v: v.tensor_tensor(T8[:], D8[:], bc8(RTs(0)), ALU.is_gt),
                [d8_b, rt_b], [t8_b])
            dvb(lambda v: v.tensor_tensor(FF8[:], D8[:], T8[:], ALU.mult),
                [d8_b, t8_b], [ff_b])
            dvb(lambda v: v.tensor_scalar(T8[:], T8[:], -BIGF, BIGF,
                                          ALU.mult, ALU.add),
                [t8_b], [t8_b])
            dvb(lambda v: v.tensor_tensor(FF8[:], FF8[:], T8[:], ALU.add),
                [ff_b, t8_b], [ff_b])
            dvb(lambda v: v.tensor_reduce(RTs(2), FF8[:], axis=AX.X, op=ALU.min),
                [ff_b], [rt_b])
            dvb(lambda v: v.tensor_tensor(T8[:], D8[:], bc8(RTs(2)), ALU.is_equal),
                [d8_b, rt_b], [t8_b])
            dvb(lambda v: v.tensor_tensor(FF8[:], T8[:], w_sb[:], ALU.mult),
                [t8_b] + w_b, [ff_b])
            dvb(lambda v: v.reduce_sum(RTs(3), FF8[:], axis=AX.X), [ff_b], [rt_b])
            s.op("dv", lambda v: v.tensor_copy(IX[:, :, 0:1], RTs(0)),
                 reads=[rt_b], writes=[ix_b])
            s.op("dv", lambda v: v.tensor_copy(IX[:, :, 1:2], RTs(2)),
                 reads=[rt_b], writes=[ix_b])
            s.op("dv", lambda v: v.tensor_copy(MT[:, :, 0:1], RTs(1)),
                 reads=[rt_b], writes=[mt_b])
            s.op("dv", lambda v: v.tensor_copy(MT[:, :, 1:2], RTs(3)),
                 reads=[rt_b], writes=[mt_b])
            s.op("dv", lambda v: v.tensor_copy(MT[:, :, 2:3], RTs(4)),
                 reads=[rt_b], writes=[mt_b])
            s.op("dv", lambda v: v.tensor_copy(
                MT[:, :, 3:4], tok_sb[:].rearrange("p (t o) -> p t o", o=1)),
                reads=[consts_b], writes=[mt_b])

            for t in range(NT):
                r = t % 2
                pay_b[r].rotate()
                s.op("dv", lambda v, r=r: v.tensor_copy(
                    pay[:, r, 0:1024], xrel[:, r, :]),
                    reads=[xrel_b[r]], writes=[pay_b[r]])
                s.op("dv", lambda v, r=r, t=t: v.tensor_copy(
                    pay[:, r, 1024:1028], MT[:, t, :]),
                    reads=[mt_b], writes=[pay_b[r]])
                for j in range(2):
                    scX_toks.append(s.op("gp", lambda g, r=r, t=t, j=j:
                                         g.indirect_dma_start(
                                             xg_d,
                                             IndirectOffsetOnAxis(IX[:, t, j:j + 1], 0),
                                             pay[:, r, :], None,
                                             bounds_check=bc_xg, oob_is_err=False),
                                         reads=[pay_b[r], ix_b],
                                         waits=[zf_toks[-1]], chan="scX%d" % r))
                wbf_b.rotate()
                s.op("dv", lambda v, t=t: v.tensor_copy(wbf[:], w_sb[:, t, :]),
                     reads=w_b, writes=[wbf_b])
                pwt_b.rotate()
                s.group("pe", [lambda t_: t_.transpose(PwT[:], wbf[:], id_sb[:])],
                        reads=[wbf_b, consts_b], writes=[pwt_b])
                wT_b[r].rotate()
                s.op("dv", lambda v, r=r: v.tensor_copy(wT[0:8, r, :], PwT[:]),
                     reads=[pwt_b], writes=[wT_b[r]])
                pb2_b.rotate()
                s.group("pe", [
                    (lambda t_, c=c, r=r: t_.matmul(
                        Pb2[:, 512 * c:512 * (c + 1)], wT[0:8, r, :],
                        b2sb[0:8, 512 * c:512 * (c + 1)], start=True, stop=True))
                    for c in range(2)],
                    reads=[wT_b[r], consts_b], writes=[pb2_b])
                o_b[r].rotate()
                for c in range(2):
                    s.op("dv", lambda v, c=c, rr=r: v.tensor_tensor(
                        o_sb[:, rr, 512 * c:512 * (c + 1)],
                        Pb2[:, 512 * c:512 * (c + 1)],
                        xrel[:, rr, 512 * c:512 * (c + 1)], ALU.add),
                        reads=[pb2_b, xrel_b[r]], writes=[o_b[r]])
                aci_toks.append(s.op("sy", lambda g, t=t, rr=r: g.dma_start(
                    acc_d[128 * t:128 * (t + 1), :], o_sb[:, rr, :]),
                    reads=[o_b[r]], chan="aci%d" % r))
                if t + 2 < NT:
                    xrel_b[r].rotate()
                    s.op("sy", lambda g, t=t, r=r: g.dma_start(
                        xrel[:, r, :], x1sp_d[128 * (t + 2):128 * (t + 3), :]),
                        writes=[xrel_b[r]], waits=[x1sp_tok[t + 2]],
                        chan="xr%d" % r)

        s.barrier(("pe", "dv", "ac"))

        Ph = es.enter_context(nc.psum_tensor("Ph", [128, 2, 1024], F32))
        Py = es.enter_context(nc.psum_tensor("Py", [128, 2, 1024], F32))
        ph_b = [B(), B()]
        py_b = [B(), B()]

        def w1_load(e, fc):
            wsl = fc % 3
            w1_b[wsl].rotate()
            s.op("gp", lambda gg, e=e, fc=fc, wsl=wsl: gg.dma_start(
                W1t[:, wsl, :, :], W1r_d[e, fc]),
                writes=[w1_b[wsl]], chan="w1%d" % wsl)

        def xga_load(e, gt):
            sl = gt % 2
            xga_b[sl].rotate()
            s.op("gp", lambda g, e=e, gt=gt, sl=sl: g.dma_start(
                xga[:, sl, :],
                xg_d[e * CAP + 128 * gt:e * CAP + 128 * (gt + 1), :]),
                writes=[xga_b[sl]], waits=scX_toks[-2:],
                chan="xga%d" % sl)

        for e in range(E):
            cur = []
            xgT_b.rotate(); wgc_b.rotate(); idxg_b.rotate()
            for gt in range(GT):
                sl = gt % 2
                if e == 0 and gt < 2:
                    xga_load(e, gt)
                s.op("dv", lambda v, gt=gt, sl=sl, e=e: v.tensor_scalar(
                    wgc[:, gt:gt + 1], xga[:, sl, 1026:1027], float(e), None,
                    ALU.is_equal),
                    reads=[xga_b[sl]], writes=[wgc_b])
                s.op("dv", lambda v, gt=gt, sl=sl: v.tensor_tensor(
                    dscr[:, 0:1], xga[:, sl, 1024:1025],
                    xga[:, sl, 1025:1026], ALU.subtract),
                    reads=[xga_b[sl]], writes=[dsc_b])
                s.op("dv", lambda v, gt=gt, sl=sl: v.tensor_tensor(
                    wgc[:, gt:gt + 1], wgc[:, gt:gt + 1],
                    dscr[:, 0:1], ALU.mult),
                    reads=[wgc_b, dsc_b], writes=[wgc_b])
                s.op("dv", lambda v, gt=gt, sl=sl: v.tensor_tensor(
                    wgc[:, gt:gt + 1], wgc[:, gt:gt + 1],
                    xga[:, sl, 1025:1026], ALU.add),
                    reads=[wgc_b, xga_b[sl]], writes=[wgc_b])
                s.op("dv", lambda v, gt=gt, sl=sl: v.tensor_copy(
                    idxg[:, gt:gt + 1], xga[:, sl, 1027:1028]),
                    reads=[xga_b[sl]], writes=[idxg_b])
                pp = gt % 2
                ph_b[pp].rotate()
                s.group("pe", [
                    (lambda t_, j=j, sl=sl, pp=pp: t_.transpose(
                        Ph[:, pp, 128 * j:128 * (j + 1)],
                        xga[:, sl, 128 * j:128 * (j + 1)], id32[:]))
                    for j in range(8)],
                    reads=[xga_b[sl], consts_b], writes=[ph_b[pp]])
                for j in range(8):
                    s.op("dv", lambda v, j=j, gt=gt, pp=pp: v.tensor_copy(
                        xgT[:, j, 128 * gt:128 * (gt + 1)],
                        Ph[:, pp, 128 * j:128 * (j + 1)]),
                        reads=[ph_b[pp]], writes=[xgT_b])
                if gt + 2 < GT:
                    xga_load(e, gt + 2)
            for g in range(NG):
                for fg in range(3):
                    if g == 0 and e > 0:
                        break
                    w1_load(e, FCG * g + fg)
                w2_b.rotate()
                for fg in range(FCG):
                    wsl2 = fg % 2
                    xrel_b[wsl2].rotate()
                    s.op("sy", lambda gg, e=e, g=g, fg=fg, wsl2=wsl2: gg.dma_start(
                        xrel[:, wsl2, :],
                        W2_d[e, 128 * (FCG * g + fg):128 * (FCG * g + fg) + 128, :]),
                        writes=[xrel_b[wsl2]], chan="w2f%d" % wsl2)
                    s.op("dv", lambda v, fg=fg, wsl2=wsl2: v.tensor_copy(
                        W2g[:, fg, :], xrel[:, wsl2, :]),
                        reads=[xrel_b[wsl2]], writes=[w2_b])
                for fg in range(FCG):
                    fc = FCG * g + fg
                    wsl = fc % 3
                    hT_b[fg].rotate()
                    psl = fg % 2
                    ph_b[psl].rotate()
                    fns = []
                    for k in range(KC):
                        for c in range(2):
                            fns.append(lambda t_, k=k, c=c, psl=psl, wsl=wsl: t_.matmul(
                                Ph[:, psl, 512 * c:512 * (c + 1)],
                                W1t[:, wsl, k, :],
                                xgT[:, k, 512 * c:512 * (c + 1)],
                                start=(k == 0), stop=(k == KC - 1)))
                    s.group("pe", fns, reads=[w1_b[wsl], xgT_b],
                            writes=[ph_b[psl]])
                    if fg + 3 < FCG:
                        w1_load(e, FCG * g + fg + 3)
                    for c in range(2):
                        s.op("ac", lambda a, fg=fg, c=c, psl=psl, e=e,
                             fc=fc: a.activation(
                                 hT[:, fg, 512 * c:512 * (c + 1)],
                                 Ph[:, psl, 512 * c:512 * (c + 1)],
                                 AF.Gelu_apprx_tanh, bias=b1c[:, e, fc:fc + 1]),
                             reads=[ph_b[psl], consts_b], writes=[hT_b[fg]])
                if g == NG - 1 and e + 1 < E:
                    xga_load(e + 1, 0)
                    xga_load(e + 1, 1)
                    for fg in range(3):
                        w1_load(e + 1, fg)
                for gt in range(GT):
                    psl = gt % 2
                    py_b[psl].rotate()
                    fns = []
                    for fg in range(FCG):
                        for c in range(2):
                            fns.append(lambda t_, fg=fg, c=c, gt=gt, psl=psl: t_.matmul(
                                Py[:, psl, 512 * c:512 * (c + 1)],
                                hT[:, fg, 128 * gt:128 * (gt + 1)],
                                W2g[:, fg, 512 * c:512 * (c + 1)],
                                start=(fg == 0), stop=(fg == FCG - 1)))
                    s.group("pe", fns,
                            reads=[w2_b] + [hT_b[q] for q in range(FCG)],
                            writes=[py_b[psl]])
                    if g == 0:
                        accg_b[gt].rotate()
                        for c in range(2):
                            s.op("dv", lambda v, gt=gt, c=c, psl=psl:
                                 v.tensor_single_scalar(
                                     accg[:, gt, 512 * c:512 * (c + 1)],
                                     Py[:, psl, 512 * c:512 * (c + 1)],
                                     wgc[:, gt:gt + 1], ALU.mult),
                                 reads=[py_b[psl], wgc_b], writes=[accg_b[gt]])
                    else:
                        for c in range(2):
                            s.op("dv", lambda v, gt=gt, c=c, psl=psl:
                                 v.scalar_tensor_tensor(
                                     accg[:, gt, 512 * c:512 * (c + 1)],
                                     Py[:, psl, 512 * c:512 * (c + 1)],
                                     wgc[:, gt:gt + 1],
                                     accg[:, gt, 512 * c:512 * (c + 1)],
                                     ALU.mult, ALU.add),
                                 reads=[py_b[psl], accg_b[gt], wgc_b],
                                 writes=[accg_b[gt]])
                        prev = (list(scA_by_e[-1]) if scA_by_e
                                else list(aci_toks[-2:]))
                        cur.append(s.op("gp", lambda g_, e=e, gt=gt:
                                        g_.indirect_dma_start(
                                            acc_d,
                                            IndirectOffsetOnAxis(idxg[:, gt:gt + 1], 0),
                                            accg[:, gt, :], None,
                                            bounds_check=bc_ac, oob_is_err=False,
                                            compute_op=ALU.add),
                                        reads=[accg_b[gt], idxg_b],
                                        waits=prev + aci_toks[-2:],
                                        chan="scA%d" % gt))
            scA_by_e.append(cur)

        out_toks = []
        for t in range(NT):
            r = t % 2
            xrel_b[r].rotate()
            s.op("sy", lambda g, t=t, rr=r: g.dma_start(
                xrel[:, rr, :], acc_d[128 * t:128 * (t + 1), :]),
                writes=[xrel_b[r]], waits=scA_by_e[-1], chan="xr%d" % r)
            ssl, bsl = stat_slot()
            stx_b[ssl].rotate(); bnx_b[bsl].rotate()
            s.op("dv", lambda v, bsl=bsl, rr=r: v.bn_stats(
                bnx[:, bsl, 0:6], xrel[:, rr, 0:512]),
                reads=[xrel_b[r]], writes=[bnx_b[bsl]])
            s.op("dv", lambda v, bsl=bsl, rr=r: v.bn_stats(
                bnx[:, bsl, 6:12], xrel[:, rr, 512:1024]),
                reads=[xrel_b[r]], writes=[bnx_b[bsl]])
            s.op("dv", lambda v, ssl=ssl, bsl=bsl: v.bn_aggr(
                stx[:, ssl, 0:2], bnx[:, bsl, 0:12]),
                reads=[bnx_b[bsl]], writes=[stx_b[ssl]])
            s.op("ac", lambda a, ssl=ssl: a.activation(
                stx[:, ssl, 2:3], stx[:, ssl, 1:2], AF.Sqrt, bias=eps[:, 0:1]),
                reads=[stx_b[ssl], consts_b], writes=[stx_b[ssl]])
            s.op("dv", lambda v, ssl=ssl: v.reciprocal(
                stx[:, ssl, 3:4], stx[:, ssl, 2:3]),
                reads=[stx_b[ssl]], writes=[stx_b[ssl]])
            o_b[r].rotate()
            s.op("dv", lambda v, rr=r, ssl=ssl: v.tensor_scalar(
                o_sb[:, rr, :], xrel[:, rr, :], stx[:, ssl, 0:1], stx[:, ssl, 3:4],
                ALU.subtract, ALU.mult),
                reads=[xrel_b[r], stx_b[ssl]], writes=[o_b[r]])
            out_toks.append(s.op("sy", lambda g, t=t, rr=r: g.dma_start(
                out_d[128 * t:128 * (t + 1), :], o_sb[:, rr, :]),
                reads=[o_b[r]], chan="o%d" % r))

        s.ops.append(("sy", lambda g: g.nop(), out_toks[-2:], None))

        semnames = {sem for _, _, waits, inc in s.ops
                    for sem, _v in (list(waits) + ([inc[0:1][0]] if False else []))}
        semnames |= {inc[0] for _, _, _, inc in s.ops if inc is not None}
        sems = {name: es.enter_context(nc.semaphore(name))
                for name in sorted(semnames)}
        block = es.enter_context(nc.Block())
        emit(block, s, sems)

    return nc


def _prep(inputs):
    def f32(a):
        return np.ascontiguousarray(np.asarray(a), dtype=np.float32)

    x = f32(inputs["L0_attn_x"])
    shared = {
        "Wq": f32(inputs["L0_attn_Wq"]), "Wk": f32(inputs["L0_attn_Wk"]),
        "Wv": f32(inputs["L0_attn_Wv"]), "Wg": f32(inputs["L0_attn_Wg"]),
        "Wo": f32(inputs["L0_attn_Wo"]),
        "bq": f32(inputs["L0_attn_bq"]).reshape(1, D),
        "bk": f32(inputs["L0_attn_bk"]).reshape(1, D),
        "bv": f32(inputs["L0_attn_bv"]).reshape(1, D),
        "bo": f32(inputs["L0_attn_bo"]).reshape(1, D),
        "gam": np.ascontiguousarray(f32(inputs["L0_attn_gamma"]).reshape(8, 128).T),
        "bet": np.ascontiguousarray(f32(inputs["L0_attn_beta"]).reshape(8, 128).T),
        "bgc": np.ascontiguousarray(f32(inputs["L0_attn_bg"]).reshape(8, 128).T),
        "rw": np.ascontiguousarray(
            f32(inputs["L0_ffn_router_w"]).T.reshape(8, 128, 8).transpose(1, 0, 2)),
        "b1c": np.ascontiguousarray(
            f32(inputs["L0_ffn_b1"]).reshape(8, 32, 128).transpose(2, 0, 1)),
        "b2": f32(inputs["L0_ffn_b2"]),
        "W1r": np.ascontiguousarray(
            f32(inputs["L0_ffn_W1"]).reshape(E, KC, 128, 32, 128)
            .transpose(0, 3, 2, 1, 4)),
        "W2": f32(inputs["L0_ffn_W2"]),
        "ident": np.eye(128, dtype=np.float32),
        "trils": np.triu(np.ones((128, 128), np.float32), k=1),
        "onesf": np.ones((128, 128), np.float32),
        "ce": np.broadcast_to(
            (np.arange(8, dtype=np.float32) * CAP)[None, None, :],
            (128, 16, 8)).copy(),
        "tokid": (np.arange(16, dtype=np.float32)[None, :] * 128
                  + np.arange(128, dtype=np.float32)[:, None]).copy(),
    }
    ii = np.arange(128, dtype=np.float64)[:, None]
    tt = np.arange(16, dtype=np.float64)[None, :]
    gidx = 128.0 * tt + ii
    shared["causal"] = (np.arange(128)[None, :] >= np.arange(128)[:, None]
                        ).astype(np.float32)
    shared["lamq"] = (LAM ** gidx / 8.0).astype(np.float32)
    shared["lamk"] = (LAM ** (-gidx)).astype(np.float32)
    return x, shared


def kernel(**inputs):
    x, shared = _prep(inputs)
    nc = build_nc()
    in_maps = [{"x": np.ascontiguousarray(x[c]), **shared} for c in range(NCORE)]
    res = run_bass_kernel_spmd(nc, in_maps, core_ids=list(range(NCORE)))
    return np.stack([res.results[c]["out"] for c in range(NCORE)], axis=0)


# version 20
# speedup vs baseline: 1.0299x; 1.0081x over previous
import contextlib
import numpy as np
import concourse.bass as bass
import concourse.mybir as mybir
from concourse.bass_core_rust import IndirectOffsetOnAxis
from concourse.bass_utils import run_bass_kernel_spmd

F32 = mybir.dt.float32
BF16 = mybir.dt.bfloat16
AF = mybir.ActivationFunctionType
ALU = mybir.AluOpType
AX = mybir.AxisListType

S, D, H, DH, E, FF = 2048, 1024, 16, 64, 8, 4096
NT = 16
KC = 8
LAM = 0.99
NCORE = 8
NG = 2
FCG = 16
CAP = 1024
GT = CAP // 128
BIGF = 1.0e9

ESEM = {"pe": "pe", "dv": "dv", "ac": "ac", "sy": "qX", "gp": "qW"}


class Buf:
    __slots__ = ("w", "r", "war")

    def __init__(self):
        self.w, self.r, self.war = [], [], []

    def rotate(self):
        self.war = list({t for t in self.w + self.r + self.war})
        self.w, self.r = [], []

    def inherit(self, others):
        for o in others:
            self.war += o.w + o.r + o.war


class Sched:
    def __init__(self):
        self.ops = []
        self.cnt = {}
        self.pending = {}

    def _tok(self, sem, n):
        c = self.cnt.get(sem, 0) + n
        self.cnt[sem] = c
        return (sem, c)

    def op(self, eng, fn, reads=(), writes=(), waits=(), n=None, chan=None):
        sem = chan or ESEM[eng]
        if n is None:
            n = 16 if eng in ("sy", "gp") else 1
        wl = list(waits)
        for b in reads:
            wl += b.w
        for b in writes:
            wl += b.war + b.r
        wl += self.pending.pop(eng, [])
        tok = self._tok(sem, n)
        self.ops.append((eng, fn, wl, (sem, n)))
        for b in reads:
            b.r.append(tok)
        for b in writes:
            b.w.append(tok)
        return tok

    def group(self, eng, fns, reads=(), writes=(), waits=()):
        sem = ESEM[eng]
        wl = list(waits)
        for b in reads:
            wl += b.w
        for b in writes:
            wl += b.war + b.r
        wl += self.pending.pop(eng, [])
        tok = self._tok(sem, 1)
        nf = len(fns)
        for i, fn in enumerate(fns):
            self.ops.append((eng, fn, wl if i == 0 else [],
                             (sem, 1) if i == nf - 1 else None))
        for b in reads:
            b.r.append(tok)
        for b in writes:
            b.w.append(tok)
        return tok

    def barrier(self, engines=("pe", "dv", "ac", "sy", "gp")):
        snap = list(self.cnt.items())
        for e in engines:
            self.pending.setdefault(e, []).extend(snap)


def emit(block, sched, sems):
    streams = {e: [] for e in ("pe", "dv", "ac", "sy", "gp")}
    for rec in sched.ops:
        streams[rec[0]].append(rec)
    for e, wl in list(sched.pending.items()):
        if wl:
            streams[e].append((e, lambda eng: eng.nop(), wl, None))
    sched.pending = {}

    def run(eng_obj, recs):
        maxw = {}
        for _, fn, waits, inc in recs:
            need = {}
            for sname, v in waits:
                if need.get(sname, 0) < v:
                    need[sname] = v
            for sname, v in need.items():
                if maxw.get(sname, 0) < v:
                    maxw[sname] = v
                    eng_obj.wait_ge(sems[sname], v)
            inst = fn(eng_obj)
            if inc is not None:
                inst.then_inc(sems[inc[0]], inc[1])

    block.tensor(lambda t: run(t, streams["pe"]))
    block.vector(lambda v: run(v, streams["dv"]))
    block.scalar(lambda a: run(a, streams["ac"]))
    block.sync(lambda sy: run(sy, streams["sy"]))
    block.gpsimd(lambda g: run(g, streams["gp"]))


def build_nc(dbg=()):
    nc = bass.Bass("TRN2", target_bir_lowering=False)

    def dt(name, shape, dtype=F32, kind="ExternalInput"):
        return nc.dram_tensor(name, shape, dtype, kind=kind).ap()

    x_d = dt("x", [S, D])
    Wq_d = dt("Wq", [D, D]); Wk_d = dt("Wk", [D, D]); Wv_d = dt("Wv", [D, D])
    Wg_d = dt("Wg", [D, D]); Wo_d = dt("Wo", [D, D])
    bq_d = dt("bq", [1, D]); bk_d = dt("bk", [1, D])
    bv_d = dt("bv", [1, D]); bo_d = dt("bo", [1, D])
    gam_d = dt("gam", [128, 8]); bet_d = dt("bet", [128, 8]); bgc_d = dt("bgc", [128, 8])
    rw_d = dt("rw", [128, 8, 8])
    b1c_d = dt("b1c", [128, 8, 32])
    b2_d = dt("b2", [8, D])
    W1r_d = dt("W1r", [E, 32, 128, 8, 128])
    W2_d = dt("W2", [E, FF, D])
    id_d = dt("ident", [128, 128])
    tri_d = dt("trils", [128, 128])
    on_d = dt("onesf", [128, 128])
    ce_d = dt("ce", [128, 16, 8])
    tok_d = dt("tokid", [128, 16])
    xg_d = dt("xg", [E * CAP, 1028], F32, "Internal")
    acc_d = dt("accd", [S, D], F32, "Internal")
    ca_d = dt("causal", [128, 128])
    lamq_d = dt("lamq", [128, 16])
    lamk_d = dt("lamk", [128, 16])
    out_d = dt("out", [S, D], F32, "ExternalOutput")
    x1sp_d = dt("x1sp", [S, D], F32, "Internal")
    dbg_d = {name: dt("dbg_" + name, [S, D], F32, "ExternalOutput") for name in dbg}

    s = Sched()
    B = Buf

    with contextlib.ExitStack() as es:
        def sb(name, shape, dtype, side="left"):
            return es.enter_context(nc.sbuf_tensor(name, shape, dtype, side=side))

        id_sb = sb("id_sb", [128, 128], BF16)
        ca_sb = sb("ca_sb", [128, 128], BF16)
        ones = sb("ones", [1, 128], BF16)
        lamq = sb("lamq_sb", [128, 16], F32)
        lamk = sb("lamk_sb", [128, 16], F32)
        eps = sb("eps_sb", [128, 1], F32)
        gam = sb("gam_sb", [128, 8], F32)
        bet = sb("bet_sb", [128, 8], F32)
        bgc = sb("bgc_sb", [128, 8], F32)
        bq_r = sb("bq_r", [1, D], BF16)
        bk_r = sb("bk_r", [1, D], BF16)
        bv_r = sb("bv_r", [1, D], BF16)
        bo_r = sb("bo_r", [1, D], BF16)
        b1c = sb("b1c_sb", [128, 8, 32], F32)
        b2sb = sb("b2_sb", [8, D], BF16)
        rw = sb("rw_sb", [128, 8, 8], BF16)
        Ssb = sb("Ssb", [64, 1024], BF16)
        x1T = sb("x1T", [128, 8, S], BF16)
        w_sb = sb("w_sb", [128, 16, 8], F32)
        stx = sb("stx", [128, 6, 4], F32)
        bnx = sb("bnx", [128, 4, 12], F32)
        tri_sb = sb("tri_sb", [128, 128], BF16)
        id32 = sb("id32", [128, 128], F32)
        on_sb = sb("on_sb", [128, 128], BF16)
        ce16 = sb("ce16", [128, 16, 8], F32)
        tok_sb = sb("tok_sb", [128, 16], F32)
        zrow = sb("zrow", [128, 1028], F32)
        flg = sb("flg", [128, 16, 8], BF16)

        bc_xg = es.enter_context(nc.gpsimd.register("bc_xg"))
        bc_ac = es.enter_context(nc.gpsimd.register("bc_ac"))

        consts_b = B()
        ssb_b = B()
        x1T_b = B()
        w_b = [B() for _ in range(NT)]
        stx_b = [B() for _ in range(6)]
        bnx_b = [B() for _ in range(4)]
        _stc = [0]

        def stat_slot():
            sl, bl = _stc[0] % 6, _stc[0] % 4
            _stc[0] += 1
            return sl, bl

        for dst_t, src in ((id_sb, id_d), (ca_sb, ca_d), (b2sb, b2_d),
                           (tri_sb, tri_d), (on_sb, on_d),
                           (bq_r, bq_d), (bk_r, bk_d), (bv_r, bv_d), (bo_r, bo_d),
                           (rw, rw_d)):
            s.op("gp", lambda g, a=dst_t, bb=src: g.dma_start(a[:], bb),
                 writes=[consts_b], chan="cG")
        for dst_t, src in ((lamq, lamq_d), (lamk, lamk_d), (gam, gam_d),
                           (bet, bet_d), (bgc, bgc_d), (b1c, b1c_d),
                           (ce16, ce_d), (tok_sb, tok_d), (id32, id_d)):
            s.op("sy", lambda g, a=dst_t, bb=src: g.dma_start(a[:], bb),
                 writes=[consts_b], chan="cS")
        s.op("gp", lambda g: g.reg_mov(bc_xg, E * CAP - 1), chan="gpr", n=1)
        s.op("gp", lambda g: g.reg_mov(bc_ac, S - 1), chan="gpr", n=1)
        s.op("dv", lambda v: v.memset(ones[:], 1.0), writes=[consts_b])
        s.op("dv", lambda v: v.memset(eps[:], 1e-5), writes=[consts_b])
        z0_tok = s.op("dv", lambda v: v.memset(zrow[:], 0.0), writes=[consts_b])
        s.op("dv", lambda v: v.memset(
            zrow[:, 1027:1028], 2048.0), writes=[consts_b],
            waits=[z0_tok])
        zf_toks = []

        with contextlib.ExitStack() as esA:
            def sbA(name, shape, dtype):
                return esA.enter_context(nc.sbuf_tensor(name, shape, dtype, side="right"))

            xtile = sbA("xtile", [128, 2, 1024], F32)
            xbf = sbA("xbf", [128, 1024], BF16)
            xTh = sbA("xTh", [128, 8, 1024], BF16)
            Vtok = sbA("Vtok", [128, 8, 1024], BF16)
            Qtok = sbA("Qtok", [128, 8, 1024], BF16)
            Ktok = sbA("Ktok", [128, 8, 1024], BF16)
            Gt = sbA("Gt", [128, 8, 1024], BF16)
            Wa = sbA("Wa", [128, 8, 1024], BF16)
            Wb = sbA("Wb", [128, 8, 1024], BF16)
            qf = sbA("qf", [64, 16, 128], BF16)
            kf = sbA("kf", [64, 16, 128], BF16)
            at = sbA("at", [128, 16, 128], BF16)
            rn = sbA("rn", [128, 1024], BF16)
            rnT = sbA("rnT", [128, 8, 128], BF16)
            ngT = sbA("ngT", [128, 8, 128], BF16)
            sum_sb = sbA("sum_sb", [128, 1024], F32)
            x1f = sbA("x1f", [128, 1024], F32)
            x1b = sbA("x1b", [128, 1024], BF16)

            STp = esA.enter_context(nc.psum_tensor("STp", [64, 1024], F32))
            Tp = esA.enter_context(nc.psum_tensor("Tp", [128, 2, 8, 128], BF16))
            PP = esA.enter_context(nc.psum_tensor("PP", [128, 2, 1024], F32))

            def SCap(sl):
                return PP[:, 0, 128 * sl:128 * (sl + 1)]

            RWap = PP[:, 1, :]

            xt_b = [B(), B()]
            xbf_b = B()
            xTh_b = [B() for _ in range(8)]
            vt_b = [B() for _ in range(8)]
            qt_b = [B() for _ in range(8)]
            kt_b = [B() for _ in range(8)]
            gt_b = [B() for _ in range(8)]
            wa_b, wb_b = B(), B()
            qf_b, kf_b, at_b = B(), B(), B()
            rn_b, rnT_b, ngT_b = B(), B(), B()
            sum_b, x1f_b, x1b_b = B(), B(), B()
            st_b = B()
            tp_b = [B(), B()]
            pp_b = [B(), B()]
            sc_b = [B(), B()]
            rw_bk = [B(), B()]
            x1sp_tok = [None] * NT

            for hf in range(2):
                wa_b.rotate(); wb_b.rotate()
                s.op("gp", lambda g: g.dma_start(
                    Wa[:], Wv_d.rearrange("(k p) d -> p k d", p=128)),
                    writes=[wa_b], chan="wV")
                s.op("gp", lambda g: g.dma_start(
                    Wb[:], Wg_d.rearrange("(k p) d -> p k d", p=128)),
                    writes=[wb_b], chan="wG")

                for tt in range(8):
                    t = 8 * hf + tt
                    sl = tt % 2
                    xt_b[sl].rotate()
                    s.op("sy", lambda g, t=t, sl=sl: g.dma_start(
                        xtile[:, sl, :], x_d[128 * t:128 * (t + 1), :]),
                        writes=[xt_b[sl]], chan="xt%d" % sl)
                    xbf_b.rotate()
                    s.op("dv", lambda v, sl=sl: v.tensor_copy(xbf[:], xtile[:, sl, :]),
                         reads=[xt_b[sl]], writes=[xbf_b])
                    pb = tp_b[tt % 2]
                    pb.rotate()
                    s.group("pe", [
                        (lambda t_, j=j, pp=tt % 2: t_.transpose(
                            Tp[:, pp, j, :], xbf[:, 128 * j:128 * (j + 1)], id_sb[:]))
                        for j in range(8)],
                        reads=[xbf_b, consts_b], writes=[pb])
                    xTh_b[tt].rotate()
                    for j in range(8):
                        s.op("ac", lambda a, j=j, tt=tt, pp=tt % 2: a.copy(
                            xTh[:, j, 128 * tt:128 * (tt + 1)], Tp[:, pp, j, :]),
                            reads=[pb], writes=[xTh_b[tt]])

                for tt in range(8):
                    sl = tt % 2
                    pp_b[sl].rotate()
                    fns = [
                        (lambda t_, sl=sl: t_.matmul(
                            PP[:, sl, 0:512], ones[:], bv_r[0:1, 0:512],
                            start=True, stop=False)),
                        (lambda t_, sl=sl: t_.matmul(
                            PP[:, sl, 512:1024], ones[:], bv_r[0:1, 512:1024],
                            start=True, stop=False)),
                    ]
                    for k in range(KC):
                        for c in range(2):
                            fns.append(lambda t_, k=k, c=c, sl=sl, tt=tt: t_.matmul(
                                PP[:, sl, 512 * c:512 * (c + 1)],
                                xTh[:, k, 128 * tt:128 * (tt + 1)],
                                Wa[:, k, 512 * c:512 * (c + 1)],
                                start=False, stop=(k == KC - 1)))
                    s.group("pe", fns, reads=[xTh_b[tt], wa_b, consts_b],
                            writes=[pp_b[sl]])
                    vt_b[tt].rotate()
                    s.op("dv", lambda v, tt=tt, sl=sl: v.tensor_copy(
                        Vtok[:, tt, :], PP[:, sl, :]),
                        reads=[pp_b[sl]], writes=[vt_b[tt]])

                for j in range(8):
                    sl = j % 2
                    pp_b[sl].rotate()
                    fns = []
                    for k in range(KC):
                        for c in range(2):
                            fns.append(lambda t_, k=k, c=c, sl=sl, j=j: t_.matmul(
                                PP[:, sl, 512 * c:512 * (c + 1)],
                                Wb[:, k, 128 * j:128 * (j + 1)],
                                xTh[:, k, 512 * c:512 * (c + 1)],
                                start=(k == 0), stop=(k == KC - 1)))
                    s.group("pe", fns,
                            reads=[wb_b, consts_b] + [xTh_b[q] for q in range(8)],
                            writes=[pp_b[sl]])
                    gt_b[j].rotate()
                    for c in range(2):
                        s.op("ac", lambda a, j=j, c=c, sl=sl: a.activation(
                            Gt[:, j, 512 * c:512 * (c + 1)],
                            PP[:, sl, 512 * c:512 * (c + 1)],
                            AF.Sigmoid, bias=bgc[:, j:j + 1]),
                            reads=[pp_b[sl], consts_b], writes=[gt_b[j]])

                wa_b.rotate()
                s.op("gp", lambda g: g.dma_start(
                    Wa[:], Wq_d.rearrange("(k p) d -> p k d", p=128)),
                    writes=[wa_b], chan="wQ")
                wb_b.rotate()
                s.op("gp", lambda g: g.dma_start(
                    Wb[:], Wk_d.rearrange("(k p) d -> p k d", p=128)),
                    writes=[wb_b], chan="wK")

                for which in ("q", "k"):
                    if which == "q":
                        warena, wt, brow, lamv, dst, dst_b = wa_b, Wa, bq_r, lamq, Qtok, qt_b
                    else:
                        warena, wt, brow, lamv, dst, dst_b = wb_b, Wb, bk_r, lamk, Ktok, kt_b
                    for tt in range(8):
                        t = 8 * hf + tt
                        sl = tt % 2
                        pp_b[sl].rotate()
                        fns = [
                            (lambda t_, sl=sl, brow=brow: t_.matmul(
                                PP[:, sl, 0:512], ones[:], brow[0:1, 0:512],
                                start=True, stop=False)),
                            (lambda t_, sl=sl, brow=brow: t_.matmul(
                                PP[:, sl, 512:1024], ones[:], brow[0:1, 512:1024],
                                start=True, stop=False)),
                        ]
                        for k in range(KC):
                            for c in range(2):
                                fns.append(lambda t_, k=k, c=c, sl=sl, tt=tt, wt=wt: t_.matmul(
                                    PP[:, sl, 512 * c:512 * (c + 1)],
                                    xTh[:, k, 128 * tt:128 * (tt + 1)],
                                    wt[:, k, 512 * c:512 * (c + 1)],
                                    start=False, stop=(k == KC - 1)))
                        s.group("pe", fns, reads=[xTh_b[tt], warena, consts_b],
                                writes=[pp_b[sl]])
                        dst_b[tt].rotate()
                        for c in range(2):
                            s.op("ac", lambda a, tt=tt, c=c, sl=sl, t=t, dst=dst,
                                 lamv=lamv: a.activation(
                                     dst[:, tt, 512 * c:512 * (c + 1)],
                                     PP[:, sl, 512 * c:512 * (c + 1)],
                                     AF.Identity, scale=lamv[:, t:t + 1]),
                                 reads=[pp_b[sl], consts_b], writes=[dst_b[tt]])

                wa_b.rotate()
                s.op("gp", lambda g: g.dma_start(
                    Wa[:], Wo_d.rearrange("(k p) d -> p k d", p=128)),
                    writes=[wa_b], chan="wO")

                for b_ in sc_b:
                    b_.inherit([pp_b[0]])
                for b_ in rw_bk:
                    b_.inherit([pp_b[1]])

                pend = [None]
                for tt in range(8):
                    i = 8 * hf + tt
                    t = i
                    sl = tt % 2
                    xt_b[sl].rotate()
                    xres_tok = s.op("sy", lambda g, t=t, sl=sl: g.dma_start(
                        xtile[:, sl, :], x_d[128 * t:128 * (t + 1), :]),
                        writes=[xt_b[sl]], chan="xt%d" % sl)
                    for zz in range(4):
                        zi = 4 * i + zz
                        zf_toks.append(s.op("sy", lambda g, zi=zi: g.dma_start(
                            xg_d[128 * zi:128 * (zi + 1), :], zrow[:]),
                            reads=[consts_b], chan="zf"))

                    qf_b.rotate()
                    for grp in range(2):
                        pb = tp_b[grp]
                        pb.rotate()
                        s.group("pe", [
                            (lambda t_, hh=hh, grp=grp, tt=tt: t_.transpose(
                                Tp[0:64, grp, hh, :],
                                Qtok[:, tt, 64 * (8 * grp + hh):64 * (8 * grp + hh) + 64],
                                id_sb[:]))
                            for hh in range(8)],
                            reads=[qt_b[tt], consts_b], writes=[pb])
                        for hh in range(8):
                            s.op("ac", lambda a, hh=hh, grp=grp: a.copy(
                                qf[:, 8 * grp + hh, :], Tp[0:64, grp, hh, :]),
                                reads=[pb], writes=[qf_b])
                    kf_b.rotate()
                    for grp in range(2):
                        pb = tp_b[grp]
                        pb.rotate()
                        s.group("pe", [
                            (lambda t_, hh=hh, grp=grp, tt=tt: t_.transpose(
                                Tp[0:64, grp, hh, :],
                                Ktok[:, tt, 64 * (8 * grp + hh):64 * (8 * grp + hh) + 64],
                                id_sb[:]))
                            for hh in range(8)],
                            reads=[kt_b[tt], consts_b], writes=[pb])
                        for hh in range(8):
                            s.op("ac", lambda a, hh=hh, grp=grp: a.copy(
                                kf[:, 8 * grp + hh, :], Tp[0:64, grp, hh, :]),
                                reads=[pb], writes=[kf_b])

                    st_b.rotate()
                    s.group("pe", [
                        (lambda t_, h=h, tt=tt, i=i: t_.matmul(
                            STp[0:64, 64 * h:64 * h + 64],
                            Ktok[:, tt, 64 * h:64 * h + 64],
                            Vtok[:, tt, 64 * h:64 * h + 64],
                            start=(i == 0 and h % 8 == 0),
                            stop=(i == NT - 1 and h == H - 1),
                            skip_group_check=True))
                        for h in range(H)],
                        reads=[kt_b[tt], vt_b[tt]], writes=[st_b])

                    at_b.rotate()
                    for hq in range(4):
                        bk = hq % 2
                        sc_b[bk].rotate()
                        s.group("pe", [
                            (lambda t_, h=h: t_.matmul(
                                SCap(h % 8), kf[:, h, :], qf[:, h, :],
                                start=True, stop=True))
                            for h in range(4 * hq, 4 * hq + 4)],
                            reads=[kf_b, qf_b], writes=[sc_b[bk]])
                        for h in range(4 * hq, 4 * hq + 4):
                            s.op("dv", lambda v, h=h: v.tensor_tensor(
                                at[:, h, :], SCap(h % 8), ca_sb[:], ALU.mult),
                                reads=[sc_b[bk], consts_b], writes=[at_b])

                    if pend[0] is not None:
                        pend[0]()
                        pend[0] = None
                    for b_ in rw_bk:
                        b_.rotate()
                    fns = [
                        (lambda t_, h=h, tt=tt: t_.matmul(
                            RWap[:, 64 * h:64 * h + 64], at[:, h, :],
                            Vtok[:, tt, 64 * h:64 * h + 64],
                            start=(h % 8 == 0), stop=False, skip_group_check=True))
                        for h in range(H)]
                    if i > 0:
                        fns += [
                            (lambda t_, h=h: t_.matmul(
                                RWap[:, 64 * h:64 * h + 64], qf[:, h, :],
                                Ssb[:, 64 * h:64 * h + 64],
                                start=False, stop=(h == H - 1), skip_group_check=True))
                            for h in range(H)]
                    s.group("pe", fns, reads=[at_b, vt_b[tt], qf_b, ssb_b],
                            writes=[rw_bk[0], rw_bk[1]])

                    ssb_b.rotate()
                    s.op("ac", lambda a: a.copy(Ssb[:], STp[0:64, :]),
                         reads=[st_b], writes=[ssb_b])

                    ssl, bsl = stat_slot()
                    stx_b[ssl].rotate(); bnx_b[bsl].rotate()
                    s.op("dv", lambda v, bsl=bsl: v.bn_stats(
                        bnx[:, bsl, 0:6], RWap[:, 0:512]),
                        reads=[rw_bk[0]], writes=[bnx_b[bsl]])
                    s.op("dv", lambda v, bsl=bsl: v.bn_stats(
                        bnx[:, bsl, 6:12], RWap[:, 512:1024]),
                        reads=[rw_bk[1]], writes=[bnx_b[bsl]])
                    s.op("dv", lambda v, ssl=ssl, bsl=bsl: v.bn_aggr(
                        stx[:, ssl, 0:2], bnx[:, bsl, 0:12]),
                        reads=[bnx_b[bsl]], writes=[stx_b[ssl]])
                    s.op("ac", lambda a, ssl=ssl: a.activation(
                        stx[:, ssl, 2:3], stx[:, ssl, 1:2], AF.Sqrt, bias=eps[:, 0:1]),
                        reads=[stx_b[ssl], consts_b], writes=[stx_b[ssl]])
                    s.op("dv", lambda v, ssl=ssl: v.reciprocal(
                        stx[:, ssl, 3:4], stx[:, ssl, 2:3]),
                        reads=[stx_b[ssl]], writes=[stx_b[ssl]])
                    rn_b.rotate()
                    s.op("dv", lambda v, ssl=ssl: v.tensor_scalar(
                        rn[:], RWap[:, :], stx[:, ssl, 0:1], stx[:, ssl, 3:4],
                        ALU.subtract, ALU.mult),
                        reads=[rw_bk[0], rw_bk[1], stx_b[ssl]], writes=[rn_b])

                    def mk_tail(tt=tt, i=i, t=t, sl=sl, xres_tok=xres_tok):
                        def _tail():
                            rnT_b.rotate()
                            for grp in range(2):
                                pb = tp_b[grp]
                                pb.rotate()
                                s.group("pe", [
                                    (lambda t_, jj=jj, grp=grp: t_.transpose(
                                        Tp[:, grp, jj, :],
                                        rn[:, 128 * (4 * grp + jj):128 * (4 * grp + jj) + 128],
                                        id_sb[:]))
                                    for jj in range(4)],
                                    reads=[rn_b, consts_b], writes=[pb])
                                for jj in range(4):
                                    j = 4 * grp + jj
                                    s.op("ac", lambda a, j=j, jj=jj, grp=grp: a.activation(
                                        rnT[:, j, :], Tp[:, grp, jj, :], AF.Identity,
                                        scale=gam[:, j:j + 1], bias=bet[:, j:j + 1]),
                                        reads=[pb, consts_b], writes=[rnT_b])
                            ngT_b.rotate()
                            for j in range(8):
                                s.op("dv", lambda v, j=j, tt=tt: v.tensor_tensor(
                                    ngT[:, j, :], rnT[:, j, :],
                                    Gt[:, j, 128 * tt:128 * (tt + 1)], ALU.mult),
                                    reads=[rnT_b, gt_b[j]], writes=[ngT_b])

                            for b_ in rw_bk:
                                b_.rotate()
                            fns = [
                                (lambda t_: t_.matmul(RWap[:, 0:512], ones[:], bo_r[0:1, 0:512],
                                                      start=True, stop=False)),
                                (lambda t_: t_.matmul(RWap[:, 512:1024], ones[:],
                                                      bo_r[0:1, 512:1024], start=True, stop=False)),
                            ]
                            for k in range(KC):
                                for c in range(2):
                                    fns.append(lambda t_, k=k, c=c: t_.matmul(
                                        RWap[:, 512 * c:512 * (c + 1)], ngT[:, k, :],
                                        Wa[:, k, 512 * c:512 * (c + 1)],
                                        start=False, stop=(k == KC - 1)))
                            s.group("pe", fns, reads=[ngT_b, wa_b, consts_b],
                                    writes=[rw_bk[0], rw_bk[1]])

                            sum_b.rotate()
                            for c in range(2):
                                s.op("dv", lambda v, c=c, sl=sl: v.tensor_tensor(
                                    sum_sb[:, 512 * c:512 * (c + 1)],
                                    RWap[:, 512 * c:512 * (c + 1)],
                                    xtile[:, sl, 512 * c:512 * (c + 1)], ALU.add),
                                    reads=[rw_bk[c], xt_b[sl]], writes=[sum_b],
                                    waits=[xres_tok])
                            ssl, bsl = stat_slot()
                            stx_b[ssl].rotate(); bnx_b[bsl].rotate()
                            s.op("dv", lambda v, bsl=bsl: v.bn_stats(
                                bnx[:, bsl, 0:6], sum_sb[:, 0:512]),
                                reads=[sum_b], writes=[bnx_b[bsl]])
                            s.op("dv", lambda v, bsl=bsl: v.bn_stats(
                                bnx[:, bsl, 6:12], sum_sb[:, 512:1024]),
                                reads=[sum_b], writes=[bnx_b[bsl]])
                            s.op("dv", lambda v, ssl=ssl, bsl=bsl: v.bn_aggr(
                                stx[:, ssl, 0:2], bnx[:, bsl, 0:12]),
                                reads=[bnx_b[bsl]], writes=[stx_b[ssl]])
                            s.op("ac", lambda a, ssl=ssl: a.activation(
                                stx[:, ssl, 2:3], stx[:, ssl, 1:2], AF.Sqrt, bias=eps[:, 0:1]),
                                reads=[stx_b[ssl], consts_b], writes=[stx_b[ssl]])
                            s.op("dv", lambda v, ssl=ssl: v.reciprocal(
                                stx[:, ssl, 3:4], stx[:, ssl, 2:3]),
                                reads=[stx_b[ssl]], writes=[stx_b[ssl]])
                            x1f_b.rotate()
                            s.op("dv", lambda v, ssl=ssl: v.tensor_scalar(
                                x1f[:], sum_sb[:], stx[:, ssl, 0:1], stx[:, ssl, 3:4],
                                ALU.subtract, ALU.mult),
                                reads=[sum_b, stx_b[ssl]], writes=[x1f_b])
                            x1sp_tok[t] = s.op("sy", lambda g, t=t: g.dma_start(
                                x1sp_d[128 * t:128 * (t + 1), :], x1f[:]),
                                reads=[x1f_b], chan="sp")
                            if "x1" in dbg_d:
                                s.op("sy", lambda g, t=t: g.dma_start(
                                    dbg_d["x1"][128 * t:128 * (t + 1), :], x1f[:]),
                                    reads=[x1f_b], chan="dbg")
                            x1b_b.rotate()
                            s.op("dv", lambda v: v.tensor_copy(x1b[:], x1f[:]),
                                 reads=[x1f_b], writes=[x1b_b])
                            for grp in range(2):
                                pb = tp_b[grp]
                                pb.rotate()
                                s.group("pe", [
                                    (lambda t_, jj=jj, grp=grp: t_.transpose(
                                        Tp[:, grp, jj, :],
                                        x1b[:, 128 * (4 * grp + jj):128 * (4 * grp + jj) + 128],
                                        id_sb[:]))
                                    for jj in range(4)],
                                    reads=[x1b_b, consts_b], writes=[pb])
                                for jj in range(4):
                                    k = 4 * grp + jj
                                    s.op("ac", lambda a, k=k, jj=jj, grp=grp, t=t: a.copy(
                                        x1T[:, k, 128 * t:128 * (t + 1)], Tp[:, grp, jj, :]),
                                        reads=[pb], writes=[x1T_b])
                        return _tail
                    pend[0] = mk_tail()
                if pend[0] is not None:
                    pend[0]()
                    pend[0] = None

        s.barrier()

        def sbF(name, shape, dtype):
            return es.enter_context(nc.sbuf_tensor(name, shape, dtype, side="right"))

        hT = sbF("hT", [128, FCG, CAP], BF16)
        W2g = sbF("W2g", [128, FCG, 1024], BF16)
        W1t = sbF("W1t", [128, 3, 8, 128], BF16)
        xga = sbF("xga", [128, 2, 1028], F32)
        xgT = sbF("xgT", [128, 8, CAP], BF16)
        accg = sbF("accg", [128, GT, 1024], F32)
        wgc = sbF("wgc", [128, GT], F32)
        idxg = sbF("idxg", [128, GT], mybir.dt.int32)
        xrel = sbF("xrel", [128, 2, 1024], F32)
        o_sb = sbF("o_sb", [128, 2, 1024], F32)
        wT = sbF("wT", [8, 2, 128], BF16)
        wbf = sbF("wbf", [128, 8], BF16)
        P8 = sbF("P8", [128, NT, 8], F32)
        T8 = sbF("T8", [128, NT, 8], F32)
        FF8 = sbF("FF8", [128, NT, 8], F32)
        D8 = sbF("D8", [128, NT, 8], F32)
        PF = sbF("PF", [128, NT, 8], F32)
        RT = sbF("RT", [128, NT, 8], F32)
        MT = sbF("MT", [128, NT, 4], F32)
        IX = sbF("IX", [128, NT, 2], mybir.dt.int32)
        dscr = sbF("dscr", [128, 4], F32)
        pay = sbF("pay", [128, 2, 1028], F32)

        hT_b = [B() for _ in range(FCG)]
        w2_b = B()
        w1_b = [B(), B(), B()]
        xga_b = [B(), B()]
        xgT_b = B()
        accg_b = [B() for _ in range(GT)]
        wgc_b, idxg_b = B(), B()
        xrel_b = [B(), B()]
        o_b = [B(), B()]
        wT_b = [B(), B()]
        wbf_b = B()
        p8_b, t8_b, ff_b, d8_b, pf_b = B(), B(), B(), B(), B()
        rt_b, mt_b, ix_b, dsc_b = B(), B(), B(), B()
        pay_b = [B(), B()]
        flg_b = B()
        scX_toks = []
        aci_toks = []
        scA_by_e = []

        with contextlib.ExitStack() as esR:
            PL = esR.enter_context(nc.psum_tensor("PL", [128, 2, 512], F32))
            Ppos = esR.enter_context(nc.psum_tensor("Ppos", [128, 2, 512], F32))
            PwT = esR.enter_context(nc.psum_tensor("PwT", [8, 128], BF16))
            Pb2 = esR.enter_context(nc.psum_tensor("Pb2", [128, 1024], F32))
            pl_b = [B(), B()]
            pp2_b = [B(), B()]
            pwt_b, pb2_b = B(), B()

            def bc8(ap):
                return ap.broadcast_to([128, NT, 8])

            for t in range(NT):
                r = t % 2
                pl_b[r].rotate()
                s.group("pe", [
                    (lambda t_, k=k, t=t, r=r: t_.matmul(
                        PL[:, r, 0:8], x1T[:, k, 128 * t:128 * (t + 1)], rw[:, k, :],
                        start=(k == 0), stop=(k == KC - 1)))
                    for k in range(KC)],
                    reads=[x1T_b, consts_b], writes=[pl_b[r]])
                s.op("dv", lambda v, t=t, r=r: v.tensor_copy(
                    P8[:, t, :], PL[:, r, 0:8]),
                    reads=[pl_b[r]], writes=[p8_b])
                if t < 2:
                    xrel_b[t].rotate()
                    s.op("sy", lambda g, t=t: g.dma_start(
                        xrel[:, t, :], x1sp_d[128 * t:128 * (t + 1), :]),
                        writes=[xrel_b[t]], waits=[x1sp_tok[t]],
                        chan="xr%d" % t)

            def dvb(fn, reads, writes):
                return s.op("dv", fn, reads=reads, writes=writes)

            RTs = lambda c: RT[:, :, c:c + 1]
            dvb(lambda v: v.reduce_max(RTs(0), P8[:], axis=AX.X), [p8_b], [rt_b])
            dvb(lambda v: v.tensor_tensor(T8[:], P8[:], bc8(RTs(0)), ALU.subtract),
                [p8_b, rt_b], [t8_b])
            s.op("ac", lambda a: a.activation(P8[:], T8[:], AF.Exp),
                 reads=[t8_b], writes=[p8_b])
            dvb(lambda v: v.reduce_sum(RTs(1), P8[:], axis=AX.X), [p8_b], [rt_b])
            dvb(lambda v: v.tensor_scalar(RTs(2), RTs(1), 1e-8, None, ALU.mult),
                [rt_b], [rt_b])
            dvb(lambda v: v.reduce_max(RTs(3), P8[:], axis=AX.X), [p8_b], [rt_b])
            dvb(lambda v: v.tensor_tensor(T8[:], P8[:], bc8(RTs(3)), ALU.is_lt),
                [p8_b, rt_b], [t8_b])
            dvb(lambda v: v.tensor_tensor(T8[:], T8[:], P8[:], ALU.mult),
                [t8_b, p8_b], [t8_b])
            dvb(lambda v: v.reduce_max(RTs(4), T8[:], axis=AX.X), [t8_b], [rt_b])
            dvb(lambda v: v.tensor_tensor(RTs(5), RTs(3), RTs(4), ALU.add),
                [rt_b], [rt_b])
            dvb(lambda v: v.tensor_tensor(RTs(6), RTs(5), RTs(2), ALU.add),
                [rt_b], [rt_b])
            dvb(lambda v: v.reciprocal(RTs(7), RTs(6)), [rt_b], [rt_b])
            dvb(lambda v: v.tensor_tensor(T8[:], P8[:], bc8(RTs(4)), ALU.is_ge),
                [p8_b, rt_b], [t8_b])
            dvb(lambda v: v.tensor_tensor(T8[:], T8[:], P8[:], ALU.mult),
                [t8_b, p8_b], [t8_b])
            for t in range(NT):
                w_b[t].rotate()
            dvb(lambda v: v.tensor_tensor(w_sb[:], T8[:], bc8(RTs(7)), ALU.mult),
                [t8_b, rt_b], w_b)
            dvb(lambda v: v.tensor_scalar(FF8[:], w_sb[:], 0.0, None, ALU.is_gt),
                w_b, [ff_b])
            flg_b.rotate()
            dvb(lambda v: v.tensor_copy(flg[:], FF8[:]), [ff_b], [flg_b])

            for t in range(NT):
                r = t % 2
                pp2_b[r].rotate()
                s.group("pe",
                        [(lambda t_, tp=tp, r=r: t_.matmul(
                            Ppos[:, r, 0:8], on_sb[:], flg[:, tp, :],
                            start=(tp == 0), stop=False, skip_group_check=True))
                         for tp in range(t)] +
                        [(lambda t_, t=t, r=r: t_.matmul(
                            Ppos[:, r, 0:8], tri_sb[:], flg[:, t, :],
                            start=(t == 0), stop=True, skip_group_check=True))],
                        reads=[flg_b, consts_b], writes=[pp2_b[r]])
                s.op("dv", lambda v, t=t, r=r: v.tensor_copy(
                    PF[:, t, :], Ppos[:, r, 0:8]),
                    reads=[pp2_b[r]], writes=[pf_b])

            dvb(lambda v: v.tensor_scalar(T8[:], PF[:], float(CAP), None, ALU.is_lt),
                [pf_b], [t8_b])
            dvb(lambda v: v.tensor_tensor(FF8[:], FF8[:], T8[:], ALU.mult),
                [ff_b, t8_b], [ff_b])
            dvb(lambda v: v.tensor_tensor(D8[:], PF[:], ce16[:], ALU.add),
                [pf_b, consts_b], [d8_b])
            dvb(lambda v: v.tensor_tensor(D8[:], D8[:], FF8[:], ALU.mult),
                [d8_b, ff_b], [d8_b])
            dvb(lambda v: v.tensor_scalar(T8[:], FF8[:], -BIGF, BIGF,
                                          ALU.mult, ALU.add),
                [ff_b], [t8_b])
            dvb(lambda v: v.tensor_tensor(D8[:], D8[:], T8[:], ALU.add),
                [d8_b, t8_b], [d8_b])
            dvb(lambda v: v.tensor_reduce(RTs(0), D8[:], axis=AX.X, op=ALU.min),
                [d8_b], [rt_b])
            dvb(lambda v: v.tensor_tensor(T8[:], D8[:], bc8(RTs(0)), ALU.is_equal),
                [d8_b, rt_b], [t8_b])
            dvb(lambda v: v.tensor_tensor(FF8[:], T8[:], ce16[:], ALU.mult),
                [t8_b, consts_b], [ff_b])
            dvb(lambda v: v.reduce_sum(RTs(4), FF8[:], axis=AX.X), [ff_b], [rt_b])
            dvb(lambda v: v.tensor_scalar(RTs(4), RTs(4), 1.0 / CAP, None, ALU.mult),
                [rt_b], [rt_b])
            dvb(lambda v: v.tensor_tensor(FF8[:], T8[:], w_sb[:], ALU.mult),
                [t8_b] + w_b, [ff_b])
            dvb(lambda v: v.reduce_sum(RTs(1), FF8[:], axis=AX.X), [ff_b], [rt_b])
            dvb(lambda v: v.tensor_tensor(T8[:], D8[:], bc8(RTs(0)), ALU.is_gt),
                [d8_b, rt_b], [t8_b])
            dvb(lambda v: v.tensor_tensor(FF8[:], D8[:], T8[:], ALU.mult),
                [d8_b, t8_b], [ff_b])
            dvb(lambda v: v.tensor_scalar(T8[:], T8[:], -BIGF, BIGF,
                                          ALU.mult, ALU.add),
                [t8_b], [t8_b])
            dvb(lambda v: v.tensor_tensor(FF8[:], FF8[:], T8[:], ALU.add),
                [ff_b, t8_b], [ff_b])
            dvb(lambda v: v.tensor_reduce(RTs(2), FF8[:], axis=AX.X, op=ALU.min),
                [ff_b], [rt_b])
            dvb(lambda v: v.tensor_tensor(T8[:], D8[:], bc8(RTs(2)), ALU.is_equal),
                [d8_b, rt_b], [t8_b])
            dvb(lambda v: v.tensor_tensor(FF8[:], T8[:], w_sb[:], ALU.mult),
                [t8_b] + w_b, [ff_b])
            dvb(lambda v: v.reduce_sum(RTs(3), FF8[:], axis=AX.X), [ff_b], [rt_b])
            s.op("dv", lambda v: v.tensor_copy(IX[:, :, 0:1], RTs(0)),
                 reads=[rt_b], writes=[ix_b])
            s.op("dv", lambda v: v.tensor_copy(IX[:, :, 1:2], RTs(2)),
                 reads=[rt_b], writes=[ix_b])
            s.op("dv", lambda v: v.tensor_copy(MT[:, :, 0:1], RTs(1)),
                 reads=[rt_b], writes=[mt_b])
            s.op("dv", lambda v: v.tensor_copy(MT[:, :, 1:2], RTs(3)),
                 reads=[rt_b], writes=[mt_b])
            s.op("dv", lambda v: v.tensor_copy(MT[:, :, 2:3], RTs(4)),
                 reads=[rt_b], writes=[mt_b])
            s.op("dv", lambda v: v.tensor_copy(
                MT[:, :, 3:4], tok_sb[:].rearrange("p (t o) -> p t o", o=1)),
                reads=[consts_b], writes=[mt_b])

            for t in range(NT):
                r = t % 2
                pay_b[r].rotate()
                s.op("dv", lambda v, r=r: v.tensor_copy(
                    pay[:, r, 0:1024], xrel[:, r, :]),
                    reads=[xrel_b[r]], writes=[pay_b[r]])
                s.op("dv", lambda v, r=r, t=t: v.tensor_copy(
                    pay[:, r, 1024:1028], MT[:, t, :]),
                    reads=[mt_b], writes=[pay_b[r]])
                for j in range(2):
                    scX_toks.append(s.op("gp", lambda g, r=r, t=t, j=j:
                                         g.indirect_dma_start(
                                             xg_d,
                                             IndirectOffsetOnAxis(IX[:, t, j:j + 1], 0),
                                             pay[:, r, :], None,
                                             bounds_check=bc_xg, oob_is_err=False),
                                         reads=[pay_b[r], ix_b],
                                         waits=[zf_toks[-1]], chan="scX%d" % r))
                wbf_b.rotate()
                s.op("dv", lambda v, t=t: v.tensor_copy(wbf[:], w_sb[:, t, :]),
                     reads=w_b, writes=[wbf_b])
                pwt_b.rotate()
                s.group("pe", [lambda t_: t_.transpose(PwT[:], wbf[:], id_sb[:])],
                        reads=[wbf_b, consts_b], writes=[pwt_b])
                wT_b[r].rotate()
                s.op("dv", lambda v, r=r: v.tensor_copy(wT[0:8, r, :], PwT[:]),
                     reads=[pwt_b], writes=[wT_b[r]])
                pb2_b.rotate()
                s.group("pe", [
                    (lambda t_, c=c, r=r: t_.matmul(
                        Pb2[:, 512 * c:512 * (c + 1)], wT[0:8, r, :],
                        b2sb[0:8, 512 * c:512 * (c + 1)], start=True, stop=True))
                    for c in range(2)],
                    reads=[wT_b[r], consts_b], writes=[pb2_b])
                o_b[r].rotate()
                for c in range(2):
                    s.op("dv", lambda v, c=c, rr=r: v.tensor_tensor(
                        o_sb[:, rr, 512 * c:512 * (c + 1)],
                        Pb2[:, 512 * c:512 * (c + 1)],
                        xrel[:, rr, 512 * c:512 * (c + 1)], ALU.add),
                        reads=[pb2_b, xrel_b[r]], writes=[o_b[r]])
                aci_toks.append(s.op("sy", lambda g, t=t, rr=r: g.dma_start(
                    acc_d[128 * t:128 * (t + 1), :], o_sb[:, rr, :]),
                    reads=[o_b[r]], chan="aci%d" % r))
                if t + 2 < NT:
                    xrel_b[r].rotate()
                    s.op("sy", lambda g, t=t, r=r: g.dma_start(
                        xrel[:, r, :], x1sp_d[128 * (t + 2):128 * (t + 3), :]),
                        writes=[xrel_b[r]], waits=[x1sp_tok[t + 2]],
                        chan="xr%d" % r)

        s.barrier(("pe", "dv", "ac"))

        Ph = es.enter_context(nc.psum_tensor("Ph", [128, 2, 1024], F32))
        Py = es.enter_context(nc.psum_tensor("Py", [128, 2, 1024], F32))
        ph_b = [B(), B()]
        py_b = [B(), B()]

        def w1_load(e, fc):
            wsl = fc % 3
            w1_b[wsl].rotate()
            s.op("gp", lambda gg, e=e, fc=fc, wsl=wsl: gg.dma_start(
                W1t[:, wsl, :, :], W1r_d[e, fc]),
                writes=[w1_b[wsl]], chan="w1%d" % wsl)

        def xga_load(e, gt):
            sl = gt % 2
            xga_b[sl].rotate()
            s.op("gp", lambda g, e=e, gt=gt, sl=sl: g.dma_start(
                xga[:, sl, :],
                xg_d[e * CAP + 128 * gt:e * CAP + 128 * (gt + 1), :]),
                writes=[xga_b[sl]], waits=scX_toks[-2:],
                chan="xga%d" % sl)

        for e in range(E):
            cur = []
            xgT_b.rotate(); wgc_b.rotate(); idxg_b.rotate()
            for gt in range(GT):
                sl = gt % 2
                if e == 0 and gt < 2:
                    xga_load(e, gt)
                s.op("dv", lambda v, gt=gt, sl=sl, e=e: v.tensor_scalar(
                    wgc[:, gt:gt + 1], xga[:, sl, 1026:1027], float(e), None,
                    ALU.is_equal),
                    reads=[xga_b[sl]], writes=[wgc_b])
                s.op("dv", lambda v, gt=gt, sl=sl: v.tensor_tensor(
                    dscr[:, 0:1], xga[:, sl, 1024:1025],
                    xga[:, sl, 1025:1026], ALU.subtract),
                    reads=[xga_b[sl]], writes=[dsc_b])
                s.op("dv", lambda v, gt=gt, sl=sl: v.tensor_tensor(
                    wgc[:, gt:gt + 1], wgc[:, gt:gt + 1],
                    dscr[:, 0:1], ALU.mult),
                    reads=[wgc_b, dsc_b], writes=[wgc_b])
                s.op("dv", lambda v, gt=gt, sl=sl: v.tensor_tensor(
                    wgc[:, gt:gt + 1], wgc[:, gt:gt + 1],
                    xga[:, sl, 1025:1026], ALU.add),
                    reads=[wgc_b, xga_b[sl]], writes=[wgc_b])
                s.op("dv", lambda v, gt=gt, sl=sl: v.tensor_copy(
                    idxg[:, gt:gt + 1], xga[:, sl, 1027:1028]),
                    reads=[xga_b[sl]], writes=[idxg_b])
                pp = gt % 2
                ph_b[pp].rotate()
                s.group("pe", [
                    (lambda t_, j=j, sl=sl, pp=pp: t_.transpose(
                        Ph[:, pp, 128 * j:128 * (j + 1)],
                        xga[:, sl, 128 * j:128 * (j + 1)], id32[:]))
                    for j in range(8)],
                    reads=[xga_b[sl], consts_b], writes=[ph_b[pp]])
                for j in range(8):
                    s.op("dv", lambda v, j=j, gt=gt, pp=pp: v.tensor_copy(
                        xgT[:, j, 128 * gt:128 * (gt + 1)],
                        Ph[:, pp, 128 * j:128 * (j + 1)]),
                        reads=[ph_b[pp]], writes=[xgT_b])
                if gt + 2 < GT:
                    xga_load(e, gt + 2)
            for g in range(NG):
                for fg in range(3):
                    if g == 0 and e > 0:
                        break
                    w1_load(e, FCG * g + fg)
                w2_b.rotate()
                for fg in range(FCG):
                    wsl2 = fg % 2
                    xrel_b[wsl2].rotate()
                    s.op("sy", lambda gg, e=e, g=g, fg=fg, wsl2=wsl2: gg.dma_start(
                        xrel[:, wsl2, :],
                        W2_d[e, 128 * (FCG * g + fg):128 * (FCG * g + fg) + 128, :]),
                        writes=[xrel_b[wsl2]], chan="w2f%d" % wsl2)
                    s.op("dv", lambda v, fg=fg, wsl2=wsl2: v.tensor_copy(
                        W2g[:, fg, :], xrel[:, wsl2, :]),
                        reads=[xrel_b[wsl2]], writes=[w2_b])
                for fg in range(FCG):
                    fc = FCG * g + fg
                    wsl = fc % 3
                    hT_b[fg].rotate()
                    psl = fg % 2
                    ph_b[psl].rotate()
                    fns = []
                    for k in range(KC):
                        for c in range(2):
                            fns.append(lambda t_, k=k, c=c, psl=psl, wsl=wsl: t_.matmul(
                                Ph[:, psl, 512 * c:512 * (c + 1)],
                                W1t[:, wsl, k, :],
                                xgT[:, k, 512 * c:512 * (c + 1)],
                                start=(k == 0), stop=(k == KC - 1)))
                    s.group("pe", fns, reads=[w1_b[wsl], xgT_b],
                            writes=[ph_b[psl]])
                    if fg + 3 < FCG:
                        w1_load(e, FCG * g + fg + 3)
                    for c in range(2):
                        s.op("ac", lambda a, fg=fg, c=c, psl=psl, e=e,
                             fc=fc: a.activation(
                                 hT[:, fg, 512 * c:512 * (c + 1)],
                                 Ph[:, psl, 512 * c:512 * (c + 1)],
                                 AF.Gelu_apprx_tanh, bias=b1c[:, e, fc:fc + 1]),
                             reads=[ph_b[psl], consts_b], writes=[hT_b[fg]])
                if g == NG - 1 and e + 1 < E:
                    xga_load(e + 1, 0)
                    xga_load(e + 1, 1)
                    for fg in range(3):
                        w1_load(e + 1, fg)
                for gt in range(GT):
                    psl = gt % 2
                    py_b[psl].rotate()
                    fns = []
                    for fg in range(FCG):
                        for c in range(2):
                            fns.append(lambda t_, fg=fg, c=c, gt=gt, psl=psl: t_.matmul(
                                Py[:, psl, 512 * c:512 * (c + 1)],
                                hT[:, fg, 128 * gt:128 * (gt + 1)],
                                W2g[:, fg, 512 * c:512 * (c + 1)],
                                start=(fg == 0), stop=(fg == FCG - 1)))
                    s.group("pe", fns,
                            reads=[w2_b] + [hT_b[q] for q in range(FCG)],
                            writes=[py_b[psl]])
                    if g == 0:
                        accg_b[gt].rotate()
                        for c in range(2):
                            s.op("dv", lambda v, gt=gt, c=c, psl=psl:
                                 v.tensor_single_scalar(
                                     accg[:, gt, 512 * c:512 * (c + 1)],
                                     Py[:, psl, 512 * c:512 * (c + 1)],
                                     wgc[:, gt:gt + 1], ALU.mult),
                                 reads=[py_b[psl], wgc_b], writes=[accg_b[gt]])
                    else:
                        for c in range(2):
                            s.op("dv", lambda v, gt=gt, c=c, psl=psl:
                                 v.scalar_tensor_tensor(
                                     accg[:, gt, 512 * c:512 * (c + 1)],
                                     Py[:, psl, 512 * c:512 * (c + 1)],
                                     wgc[:, gt:gt + 1],
                                     accg[:, gt, 512 * c:512 * (c + 1)],
                                     ALU.mult, ALU.add),
                                 reads=[py_b[psl], accg_b[gt], wgc_b],
                                 writes=[accg_b[gt]])
                        prev = (list(scA_by_e[-1]) if scA_by_e
                                else list(aci_toks[-2:]))
                        cur.append(s.op("gp", lambda g_, e=e, gt=gt:
                                        g_.indirect_dma_start(
                                            acc_d,
                                            IndirectOffsetOnAxis(idxg[:, gt:gt + 1], 0),
                                            accg[:, gt, :], None,
                                            bounds_check=bc_ac, oob_is_err=False,
                                            compute_op=ALU.add),
                                        reads=[accg_b[gt], idxg_b],
                                        waits=prev + aci_toks[-2:],
                                        chan="scA%d" % gt))
            scA_by_e.append(cur)

        out_toks = []
        for t in range(NT):
            r = t % 2
            xrel_b[r].rotate()
            s.op("sy", lambda g, t=t, rr=r: g.dma_start(
                xrel[:, rr, :], acc_d[128 * t:128 * (t + 1), :]),
                writes=[xrel_b[r]], waits=scA_by_e[-1], chan="xr%d" % r)
            ssl, bsl = stat_slot()
            stx_b[ssl].rotate(); bnx_b[bsl].rotate()
            s.op("dv", lambda v, bsl=bsl, rr=r: v.bn_stats(
                bnx[:, bsl, 0:6], xrel[:, rr, 0:512]),
                reads=[xrel_b[r]], writes=[bnx_b[bsl]])
            s.op("dv", lambda v, bsl=bsl, rr=r: v.bn_stats(
                bnx[:, bsl, 6:12], xrel[:, rr, 512:1024]),
                reads=[xrel_b[r]], writes=[bnx_b[bsl]])
            s.op("dv", lambda v, ssl=ssl, bsl=bsl: v.bn_aggr(
                stx[:, ssl, 0:2], bnx[:, bsl, 0:12]),
                reads=[bnx_b[bsl]], writes=[stx_b[ssl]])
            s.op("ac", lambda a, ssl=ssl: a.activation(
                stx[:, ssl, 2:3], stx[:, ssl, 1:2], AF.Sqrt, bias=eps[:, 0:1]),
                reads=[stx_b[ssl], consts_b], writes=[stx_b[ssl]])
            s.op("dv", lambda v, ssl=ssl: v.reciprocal(
                stx[:, ssl, 3:4], stx[:, ssl, 2:3]),
                reads=[stx_b[ssl]], writes=[stx_b[ssl]])
            o_b[r].rotate()
            s.op("dv", lambda v, rr=r, ssl=ssl: v.tensor_scalar(
                o_sb[:, rr, :], xrel[:, rr, :], stx[:, ssl, 0:1], stx[:, ssl, 3:4],
                ALU.subtract, ALU.mult),
                reads=[xrel_b[r], stx_b[ssl]], writes=[o_b[r]])
            out_toks.append(s.op("sy", lambda g, t=t, rr=r: g.dma_start(
                out_d[128 * t:128 * (t + 1), :], o_sb[:, rr, :]),
                reads=[o_b[r]], chan="o%d" % r))

        s.ops.append(("sy", lambda g: g.nop(), out_toks[-2:], None))

        semnames = {sem for _, _, waits, inc in s.ops
                    for sem, _v in (list(waits) + ([inc[0:1][0]] if False else []))}
        semnames |= {inc[0] for _, _, _, inc in s.ops if inc is not None}
        sems = {name: es.enter_context(nc.semaphore(name))
                for name in sorted(semnames)}
        block = es.enter_context(nc.Block())
        emit(block, s, sems)

    return nc


def _prep(inputs):
    def f32(a):
        return np.ascontiguousarray(np.asarray(a), dtype=np.float32)

    x = f32(inputs["L0_attn_x"])
    shared = {
        "Wq": f32(inputs["L0_attn_Wq"]), "Wk": f32(inputs["L0_attn_Wk"]),
        "Wv": f32(inputs["L0_attn_Wv"]), "Wg": f32(inputs["L0_attn_Wg"]),
        "Wo": f32(inputs["L0_attn_Wo"]),
        "bq": f32(inputs["L0_attn_bq"]).reshape(1, D),
        "bk": f32(inputs["L0_attn_bk"]).reshape(1, D),
        "bv": f32(inputs["L0_attn_bv"]).reshape(1, D),
        "bo": f32(inputs["L0_attn_bo"]).reshape(1, D),
        "gam": np.ascontiguousarray(f32(inputs["L0_attn_gamma"]).reshape(8, 128).T),
        "bet": np.ascontiguousarray(f32(inputs["L0_attn_beta"]).reshape(8, 128).T),
        "bgc": np.ascontiguousarray(f32(inputs["L0_attn_bg"]).reshape(8, 128).T),
        "rw": np.ascontiguousarray(
            f32(inputs["L0_ffn_router_w"]).T.reshape(8, 128, 8).transpose(1, 0, 2)),
        "b1c": np.ascontiguousarray(
            f32(inputs["L0_ffn_b1"]).reshape(8, 32, 128).transpose(2, 0, 1)),
        "b2": f32(inputs["L0_ffn_b2"]),
        "W1r": np.ascontiguousarray(
            f32(inputs["L0_ffn_W1"]).reshape(E, KC, 128, 32, 128)
            .transpose(0, 3, 2, 1, 4)),
        "W2": f32(inputs["L0_ffn_W2"]),
        "ident": np.eye(128, dtype=np.float32),
        "trils": np.triu(np.ones((128, 128), np.float32), k=1),
        "onesf": np.ones((128, 128), np.float32),
        "ce": np.broadcast_to(
            (np.arange(8, dtype=np.float32) * CAP)[None, None, :],
            (128, 16, 8)).copy(),
        "tokid": (np.arange(16, dtype=np.float32)[None, :] * 128
                  + np.arange(128, dtype=np.float32)[:, None]).copy(),
    }
    ii = np.arange(128, dtype=np.float64)[:, None]
    tt = np.arange(16, dtype=np.float64)[None, :]
    gidx = 128.0 * tt + ii
    shared["causal"] = (np.arange(128)[None, :] >= np.arange(128)[:, None]
                        ).astype(np.float32)
    shared["lamq"] = (LAM ** gidx / 8.0).astype(np.float32)
    shared["lamk"] = (LAM ** (-gidx)).astype(np.float32)
    return x, shared


def kernel(**inputs):
    x, shared = _prep(inputs)
    nc = build_nc()
    in_maps = [{"x": np.ascontiguousarray(x[c]), **shared} for c in range(NCORE)]
    res = run_bass_kernel_spmd(nc, in_maps, core_ids=list(range(NCORE)))
    return np.stack([res.results[c]["out"] for c in range(NCORE)], axis=0)
